# version 20
# baseline (speedup 1.0000x reference)
import numpy as np
import ml_dtypes
import concourse.bass as bass
import concourse.mybir as mybir
from concourse.bass_core_rust import IndirectOffsetOnAxis
from concourse.bass_utils import run_bass_kernel_spmd

F32 = mybir.dt.float32
BF16 = mybir.dt.bfloat16
U32 = mybir.dt.uint32
AF = mybir.ActivationFunctionType
ALU = mybir.AluOpType
AX = mybir.AxisListType

NCORES = 8
H = 1024
NH, HD = 16, 64
WIN = 256
TCORE = 2048
TK = TCORE + WIN
NTILE = TCORE // 128
NKT = TK // 128
NEXP, TOPK, CAP, INTER = 64, 8, 256, 768
EPL = NEXP // NCORES
SLOTS = EPL * CAP
TALL = NCORES * TCORE
EPS = 1e-5
BIG = 1.0e6

ENGS = ("T", "V", "S", "P", "Y")


class Tracker:

    def __init__(self, nc):
        self.nc = nc
        self._sem_cm = {e: nc.alloc_semaphore(f"cm_{e}") for e in ENGS}
        self._sem_dma = {e: nc.alloc_semaphore(f"dma_{e}") for e in ENGS}
        self.cnt = {e: 0 for e in ENGS}
        self.dcnt = {e: 0 for e in ENGS}
        self.last_w = {}
        self.readers = {}
        self.wm = {}
        self.ops = {e: [] for e in ENGS}

    def _sem(self, kind, e):
        return self._sem_cm[e] if kind == "c" else self._sem_dma[e]

    def _need(self, waiter, kind, e, val):
        if val <= 0:
            return None
        if kind == "c" and e == waiter == "T":
            return None
        k = (waiter, (kind, e))
        if self.wm.get(k, 0) >= val:
            return None
        self.wm[k] = val
        return (kind, e, val)

    def op(self, eng, emit, reads=(), writes=(), dma=False):
        waits = []
        for k in reads:
            w = self.last_w.get(k)
            if w is not None:
                n = self._need(eng, w[0], w[1], w[2])
                if n:
                    waits.append(n)
        for k in writes:
            w = self.last_w.get(k)
            if w is not None:
                n = self._need(eng, w[0], w[1], w[2])
                if n:
                    waits.append(n)
            for (kind, e), val in self.readers.get(k, {}).items():
                n = self._need(eng, kind, e, val)
                if n:
                    waits.append(n)
        if dma:
            self.dcnt[eng] += 1
            ev = ("d", eng, self.dcnt[eng] * 16)
        else:
            self.cnt[eng] += 1
            ev = ("c", eng, self.cnt[eng])
        for k in writes:
            self.last_w[k] = ev
            self.readers[k] = {}
        for k in reads:
            self.readers.setdefault(k, {})[(ev[0], eng)] = ev[2]
        self.ops[eng].append((waits, emit, ev))
        return ev

    def barrier(self, skip=None):
        skip = skip or {}
        snap = [("c", e, self.cnt[e] - skip.get(e, 0)) for e in ENGS] + \
               [("d", e, self.dcnt[e] * 16) for e in ENGS]
        for eng in ENGS:
            waits = []
            for kind, e, val in snap:
                n = self._need(eng, kind, e, val)
                if n:
                    waits.append(n)
            if waits:
                self.ops[eng].append((waits, None, None))

    def wait_all_dma(self):
        for eng in ENGS:
            n = self._need(eng, "d", eng, self.dcnt[eng] * 16)
            if n:
                self.ops[eng].append(([n], None, None))

    def build(self, block):
        tr = self

        def mk(e_name):
            def run(engine):
                for waits, emit, ev in tr.ops[e_name]:
                    for kind, pe, val in waits:
                        engine.wait_ge(tr._sem(kind, pe), val)
                    if emit is None:
                        continue
                    inst = emit(engine)
                    if ev is not None:
                        if ev[0] == "d":
                            inst.then_inc(tr._sem_dma[e_name], 16)
                        else:
                            inst.then_inc(tr._sem_cm[e_name], 1)
            return run

        block.tensor(mk("T"))
        block.vector(mk("V"))
        block.scalar(mk("S"))
        block.gpsimd(mk("P"))
        block.sync(mk("Y"))


def build_program():
    nc = bass.Bass("TRN2", target_bir_lowering=False, num_devices=NCORES)
    RG = [list(range(NCORES))]

    def din(name, shape, dt=F32):
        return nc.dram_tensor(name, shape, dt, kind="ExternalInput").ap()

    x_tm = din("x_tm", [TK, H])
    x_fmT = din("x_fmT", [H, TK])
    wqT = din("wqT", [H, H]); wkT = din("wkT", [H, H])
    wvT = din("wvT", [H, H]); woT = din("woT", [H, H])
    bq_fm = din("bq_fm", [128, 8]); bk_fm = din("bk_fm", [128, 8])
    bv_row = din("bv_row", [1, H]); bo_row = din("bo_row", [1, H])
    lnqw2 = din("lnqw2", [1, 128]); lnqb2 = din("lnqb2", [1, 128])
    lnkw2 = din("lnkw2", [1, 128]); lnkb2 = din("lnkb2", [1, 128])
    gateT = din("gateT", [H, NEXP])
    wgT = din("wgT", [EPL, H, INTER])
    wuT = din("wuT", [EPL, H, INTER])
    wdT = din("wdT", [EPL, INTER, H])
    tri_in = din("tri_in", [128, 128])
    ident_in = din("ident_in", [128, 128])
    tvals = din("tvals", [1, 2])
    tids = din("tids", [128, NTILE])
    coremask = din("coremask", [8, 1])
    lmask9 = din("lmask9", [8, 9])
    sel64 = din("sel64", [64, 8])
    ones_row = din("ones_row", [1, 512])
    erow_in = din("erow_in", [1, NEXP])
    eighth_row = din("eighth_row", [1, 512])

    y_out = nc.dram_tensor("y", [TCORE, H], F32, kind="ExternalOutput").ap()

    h32_d = nc.dram_tensor("h32_d", [TCORE, H], F32, kind="Internal").ap()
    hbf_in = nc.dram_tensor("hbf_in", [TCORE, H], BF16, kind="Internal").ap()
    hall = nc.dram_tensor("hall", [TALL + 128, H], BF16, kind="Internal",
                          addr_space="Shared").ap()
    tot_in = nc.dram_tensor("tot_in", [1, NEXP], F32, kind="Internal").ap()
    tot_out = nc.dram_tensor("tot_out", [NCORES, NEXP], F32, kind="Internal",
                             addr_space="Shared").ap()
    TSZ = TALL + 256
    ts_in = nc.dram_tensor("ts_in", [TSZ, 2], F32, kind="Internal").ap()
    ts_all = nc.dram_tensor("ts_all", [8 * TSZ, 2], F32, kind="Internal",
                            addr_space="Shared").ap()
    yexp_in = nc.dram_tensor("yexp_in", [SLOTS, H], BF16, kind="Internal").ap()
    yall = nc.dram_tensor("yall", [TALL + 128, H], BF16, kind="Internal",
                          addr_space="Shared").ap()

    tr = Tracker(nc)
    from contextlib import ExitStack
    with ExitStack() as st:
        def mksb(stk):
            def sbf(name, shape, dt=F32):
                return stk.enter_context(nc.sbuf_tensor(name, shape, dt))
            return sbf
        sb = mksb(st)

        identb = sb("identb", [128, 128], BF16)
        identf = sb("identf", [128, 128])
        trib = sb("trib", [128, 128], BF16)
        masks = sb("masks", [128, 3, 384], BF16)
        bqc = sb("bqc", [128, 8]); bkc = sb("bkc", [128, 8])
        bvrep = sb("bvrep", [128, H]); borep = sb("borep", [128, H])
        gtsb = sb("gtsb", [128, 8, NEXP])
        ALq = sb("ALq", [16, 8, 128], BF16); BLq = sb("BLq", [17, 8, 128], BF16)
        ALk = sb("ALk", [16, 8, 128], BF16); BLk = sb("BLk", [17, 8, 128], BF16)
        E16 = sb("E16", [128, 8, 16], BF16)
        iota64r = sb("iota64r", [128, 64])
        iota16r = sb("iota16r", [128, 16])
        tidsb = sb("tidsb", [128, NTILE])
        tids1 = sb("tids1", [128, NTILE])
        cmsk = sb("cmsk", [8, 1])
        onesrow_f = sb("onesrow_f", [1, 128])
        onescol_b = sb("onescol_b", [128, 1], BF16)
        selcol = sb("selcol", [128, 1])
        sel8col = sb("sel8col", [128, 1])
        trashcol = sb("trashcol", [128, 1])
        erowb = sb("erowb", [128, NEXP])
        erowk8 = sb("erowk8", [128, 8])
        logits = sb("logits", [128, NTILE, 64])
        idx8a = sb("idx8a", [128, NTILE, 8])
        w8na = sb("w8na", [128, NTILE, 8])
        lposk8 = sb("lposk8", [128, NTILE, 8])
        lprefx = sb("lprefx", [1, 64])
        coreoff = sb("coreoff", [1, 64])
        offs_all = sb("offs_all", [128, NTILE, 8], U32)
        tw_my = sb("tw_my", [128, 16, 2])
        hidxu = sb("hidxu", [128, 16], U32)
        scr = sb("scr", [128, H])
        lnred = sb("lnred", [128, 4])
        xres = sb("xres", [128, 3, H])
        nmax = sb("nmax", [128, 1]); den = sb("den", [128, 1]); rden = sb("rden", [128, 1])
        Esb = sb("Esb", [128, 64]); Ec = sb("Ec", [128, 64])
        w8v = sb("w8v", [128, 8]); idx8u = sb("idx8u", [128, 8], U32)
        Rb = sb("Rb", [128, 64], BF16)
        ws = sb("ws", [128, 1]); rws = sb("rws", [128, 1])
        ctot = sb("ctot", [1, 64]); totall = sb("totall", [8, 64])
        posg = sb("posg", [128, 64]); ohk = sb("ohk", [128, 64])
        junk64 = sb("junk64", [128, 64])
        posk8 = sb("posk8", [128, 8]); valid8 = sb("valid8", [128, 8])
        idx256 = sb("idx256", [128, 8]); soff8 = sb("soff8", [128, 8])
        coff8 = sb("coff8", [128, 8]); tmp8 = sb("tmp8", [128, 8])
        pays = [sb(f"pays{i}", [128, 8, 2]) for i in range(4)]
        soffu = [sb(f"soffu{i}", [128, 8], U32) for i in range(4)]
        idx8f = sb("idx8f", [128, 8])
        hidxf = sb("hidxf", [128, 16]); hmsk = sb("hmsk", [128, 16])

        pb = [st.enter_context(nc.psum_tensor(f"pb{i}", [128, 512], F32))
              for i in range(8)]

        def pbf(i, cols):
            return pb[i][:].bitcast(BF16)[:, 0:cols]

        def bcast_row(dst, src_ap, M, N, rkey, wkey):
            for c0 in range(0, N, 512):
                cw = min(512, N - c0)
                def em_bm(E, c0=c0, cw=cw, M=M):
                    return E.matmul(pb[7][0:M, 0:cw], onesrow_f[0:1, 0:M],
                                    src_ap[0:1, c0:c0 + cw], start=True, stop=True)
                tr.op("T", em_bm, reads=[rkey, "onesrow_f"], writes=["pb7"])
                def em_bc(E, c0=c0, cw=cw, M=M, dst=dst):
                    return E.tensor_copy(dst[0:M, c0:c0 + cw], pb[7][0:M, 0:cw])
                tr.op("V", em_bc, reads=["pb7"], writes=[wkey])

        block = st.enter_context(nc.Block())

        tr.op("P", lambda E: E.dma_start(out=identb[:], in_=ident_in), writes=["identb"], dma=True)
        tr.op("V", lambda E: E.memset(onesrow_f[:], 1.0), writes=["onesrow_f"])
        tr.op("V", lambda E: E.memset(onescol_b[:], 1.0), writes=["onescol_b"])
        tr.op("Y", lambda E: E.dma_start(out=identf[:], in_=ident_in), writes=["identf"], dma=True)
        tr.op("P", lambda E: E.dma_start(out=trib[:], in_=tri_in), writes=["trib"], dma=True)
        tr.op("Y", lambda E: E.dma_start(out=bqc[:], in_=bq_fm), writes=["bqc"], dma=True)
        tr.op("Y", lambda E: E.dma_start(out=bkc[:], in_=bk_fm), writes=["bkc"], dma=True)
        tr.op("Y", lambda E: E.dma_start(out=bvrep[0:1, :], in_=bv_row), writes=["bvrep"], dma=True)
        tr.op("Y", lambda E: E.dma_start(out=borep[0:1, :], in_=bo_row), writes=["borep"], dma=True)
        bcast_row(bvrep, bvrep, 128, H, "bvrep", "bvrep")
        bcast_row(borep, borep, 128, H, "borep", "borep")
        tr.op("Y", lambda E: E.dma_start(out=gtsb[:], in_=gateT.rearrange("(a p) e -> p a e", p=128)),
              writes=["gtsb"], dma=True)
        tr.op("Y", lambda E: E.dma_start(out=tidsb[:], in_=tids), writes=["tidsb"], dma=True)
        tr.op("V", lambda E: E.tensor_scalar(tids1[:], tidsb[:], 1.0, None, ALU.add),
              reads=["tidsb"], writes=["tids1"])
        tr.op("Y", lambda E: E.dma_start(out=cmsk[:], in_=coremask), writes=["cmsk"], dma=True)
        tr.op("P", lambda E: E.iota(iota64r[:], [[1, 64]], base=0, channel_multiplier=0,
                                    allow_small_or_imprecise_dtypes=True), writes=["iota64r"])
        tr.op("P", lambda E: E.iota(iota16r[:], [[1, 16]], base=0, channel_multiplier=0,
                                    allow_small_or_imprecise_dtypes=True), writes=["iota16r"])
        tr.op("V", lambda E: E.memset(scr[:, 0:260], 0.0), writes=["scr0"])
        tr.op("P", lambda E: E.dma_start(
            out=ts_in.rearrange("(p a) c -> p (a c)", p=128), in_=scr[:, 0:260]),
            reads=["scr0"], writes=["tsz"], dma=True)
        tr.op("P", lambda E: E.iota(selcol[:], [[1, 1]], base=0, channel_multiplier=1,
                                    allow_small_or_imprecise_dtypes=True), writes=["selcol"])
        tr.op("V", lambda E: E.tensor_scalar(selcol[:], selcol[:], 63.5, None, ALU.is_gt),
              reads=["selcol"], writes=["selcol"])
        tr.op("V", lambda E: E.tensor_scalar(sel8col[:], selcol[:], 8.0, None, ALU.mult),
              reads=["selcol"], writes=["sel8col"])
        tr.op("P", lambda E: E.iota(trashcol[:], [[1, 1]], base=TALL, channel_multiplier=1,
                                    allow_small_or_imprecise_dtypes=True), writes=["trashcol"])
        tr.op("Y", lambda E: E.dma_start(out=erowb[0:1, :], in_=erow_in), writes=["erowb"], dma=True)
        bcast_row(erowb, erowb, 128, NEXP, "erowb", "erowb")
        for ft in range(8):
            def em_e16(E, ft=ft):
                return E.tensor_scalar(E16[:, ft, :], iota16r[:], float(ft), sel8col[:, 0:1],
                                       ALU.subtract, ALU.is_equal)
            tr.op("V", em_e16, reads=["iota16r", "sel8col"], writes=["E16"])
        tr.op("S", lambda E: E.activation(lnred[0:1, 0:1], onesrow_f[0:1, 0:1], AF.Exp),
              reads=["onesrow_f"], writes=["lnred"])

        with ExitStack() as st0:
            sb0 = mksb(st0)
            mband = sb0("mband", [128, 384])
            mtmp = sb0("mtmp", [128, 384])
            miotc = sb0("miotc", [128, 384])
            tvsb = sb0("tvsb", [128, 2])
            tr.op("Y", lambda E: E.dma_start(out=tvsb[0:1, :], in_=tvals), writes=["tvsb"], dma=True)
            bcast_row(tvsb, tvsb, 128, 2, "tvsb", "tvsb")
            tr.op("P", lambda E: E.iota(mband[:], [[1, 384]], base=0, channel_multiplier=-1,
                                        allow_small_or_imprecise_dtypes=True), writes=["mband"])
            tr.op("P", lambda E: E.iota(miotc[:], [[1, 384]], base=0, channel_multiplier=0,
                                        allow_small_or_imprecise_dtypes=True), writes=["miotc"])
            tr.op("V", lambda E: E.tensor_scalar(mtmp[:], mband[:], -0.5, -1.0e30,
                                                 ALU.is_lt, ALU.mult),
                  reads=["mband"], writes=["mtmp"])
            tr.op("V", lambda E: E.tensor_scalar(mband[:], mband[:], WIN + 0.5, -1.0e30,
                                                 ALU.is_gt, ALU.mult),
                  reads=["mband"], writes=["mband"])
            tr.op("V", lambda E: E.tensor_tensor(masks[:, 2, :], mband[:], mtmp[:], ALU.add),
                  reads=["mband", "mtmp"], writes=["masks2"])
            for j in (0, 1):
                def em_sp(E, j=j):
                    return E.tensor_scalar(mtmp[:], miotc[:], tvsb[:, j:j + 1], -1.0e30,
                                           ALU.is_lt, ALU.mult)
                tr.op("V", em_sp, reads=["miotc", "tvsb"], writes=["mtmp"])
                def em_m(E, j=j):
                    return E.tensor_tensor(masks[:, j, :], masks[:, 2, :], mtmp[:], ALU.add)
                tr.op("V", em_m, reads=["masks2", "mtmp"], writes=[f"masks{j}"])

            g2q = sb0("g2q", [1, 128]); g2k = sb0("g2k", [1, 128])
            gb16 = sb0("gb16", [16, 128])
            tgt16 = sb0("tgt16", [16, 128])
            selb16 = sb0("selb16", [16, 128])
            iota16c = sb0("iota16c", [16, 1])
            selrow = sb0("selrow", [1, 128])
            tr.op("Y", lambda E: E.dma_start(out=g2q[:], in_=lnqw2), writes=["g2q"], dma=True)
            tr.op("Y", lambda E: E.dma_start(out=g2k[:], in_=lnkw2), writes=["g2k"], dma=True)
            tr.op("V", lambda E: E.tensor_scalar(g2q[:], g2q[:], 0.125, None, ALU.mult),
                  reads=["g2q"], writes=["g2q"])
            tr.op("P", lambda E: E.iota(iota16c[:], [[1, 1]], base=0, channel_multiplier=1,
                                        allow_small_or_imprecise_dtypes=True), writes=["iota16c"])
            tr.op("P", lambda E: E.iota(selrow[:], [[1, 128]], base=0, channel_multiplier=0,
                                        allow_small_or_imprecise_dtypes=True), writes=["selrow"])
            tr.op("V", lambda E: E.tensor_scalar(selrow[:], selrow[:], 63.5, 8.0,
                                                 ALU.is_gt, ALU.mult),
                  reads=["selrow"], writes=["selrow"])
            bcast_row(selb16, selrow, 16, 128, "selrow", "selb16")
            for mat, AL, BL, g2, gkey, beta_in in (
                    ("q", ALq, BLq, g2q, "g2q", lnqb2),
                    ("k", ALk, BLk, g2k, "g2k", lnkb2)):
                bcast_row(gb16, g2, 16, 128, gkey, "gb16")
                for ft in range(8):
                    def em_tg(E, ft=ft):
                        return E.tensor_scalar(tgt16[:], selb16[:], float(ft), None, ALU.add)
                    tr.op("V", em_tg, reads=["selb16"], writes=["tgt16"])
                    def em_eq(E):
                        return E.tensor_scalar(tgt16[:], tgt16[:], iota16c[:, 0:1], None,
                                               ALU.is_equal)
                    tr.op("V", em_eq, reads=["tgt16", "iota16c"], writes=["tgt16"])
                    def em_al(E, AL=AL, ft=ft):
                        return E.tensor_tensor(AL[:, ft, :], tgt16[:], gb16[:], ALU.mult)
                    tr.op("V", em_al, reads=["tgt16", "gb16"], writes=[f"AL{mat}"])
                    def em_bl(E, BL=BL, ft=ft):
                        return E.tensor_tensor(BL[0:16, ft, :], tgt16[:], gb16[:], ALU.mult)
                    tr.op("V", em_bl, reads=["tgt16", "gb16"], writes=[f"BL{mat}"])
                    def em_blb(E, BL=BL, ft=ft, beta_in=beta_in):
                        return E.dma_start(out=BL[16:17, ft, :], in_=beta_in)
                    tr.op("P", em_blb, writes=[f"BL{mat}"], dma=True)
            tr.barrier()

        with ExitStack() as st1:
            sb1 = mksb(st1)
            Qb = sb1("Qb", [128, 8, TCORE], BF16)
            Kb = sb1("Kb", [128, 8, TK], BF16)
            Vb = sb1("Vb", [128, NKT, H], BF16)

            CH5 = [(0, 512), (512, 512), (1024, 512), (1536, 512), (2048, 256)]

            with ExitStack() as st1a:
                sb1a = mksb(st1a)
                Wbuf = sb1a("Wbuf", [128, 8, H], BF16)
                xbuf = [sb1a(f"xbuf{i}", [128, 8, 512], BF16) for i in range(2)]
                q2buf = sb1a("q2buf", [128, 8, 512], BF16)
                sta = sb1a("sta", [16, 512])
                stbx = sb1a("stbx", [16, 512])
                m2b = sb1a("m2b", [16, 512])
                rstd_b = [sb1a(f"rstd_b{i}", [16, 512], BF16) for i in range(2)]
                rhsB_b = [sb1a(f"rhsB_b{i}", [17, 512], BF16) for i in range(2)]

                def xchunk_load(ci, pp, key):
                    c0, cw = CH5[ci]
                    def em_x(E, c0=c0, cw=cw, pp=pp):
                        return E.dma_start(
                            out=xbuf[pp][:, :, 0:cw],
                            in_=x_fmT.rearrange("(a p) t -> p a t", p=128)[:, :, c0:c0 + cw])
                    tr.op("P", em_x, writes=[key], dma=True)

                def qk_pass(wT_in, out_sb, okey, bias_col, bias_key, mat, AL, BL,
                            is_q, row16_const):
                    tr.op("P", lambda E, w=wT_in: E.dma_start(
                        out=Wbuf[:], in_=w.rearrange("(a p) f -> p a f", p=128)),
                        writes=["Wbuf"], dma=True)
                    for par in range(2):
                        def em_c16(E, rc=row16_const, par=par):
                            return E.dma_start(out=rhsB_b[par][16:17, :], in_=rc)
                        tr.op("P", em_c16, writes=[f"rhsBc{par}"], dma=True)
                    xchunk_load(0, 0, f"xb0{mat}")
                    B4 = (0, 1, 6, 7)
                    chunks = []
                    for ci in range(len(CH5)):
                        c0, cw = CH5[ci]
                        if is_q:
                            xs = WIN if ci == 0 else 0
                            qd = c0 + xs - WIN
                            qw = cw - xs
                        else:
                            xs, qd, qw = 0, c0, cw
                        if qw > 0:
                            chunks.append((ci, xs, qd, qw))

                    def phase1(idx):
                        ci, xs, qd, qw = chunks[idx]
                        pp = ci % 2
                        xk = f"xb{pp}{mat}"
                        if ci + 1 < len(CH5):
                            xchunk_load(ci + 1, (ci + 1) % 2,
                                        f"xb{(ci + 1) % 2}{mat}")
                        for ft in range(8):
                            bank = B4[ft % 4]
                            def em_mm(E, ft=ft, bank=bank, pp=pp, xs=xs, qw=qw):
                                inst = None
                                for fi in range(8):
                                    inst = E.matmul(pb[bank][:, 0:qw],
                                                    Wbuf[:, fi,
                                                         ft * 128:(ft + 1) * 128],
                                                    xbuf[pp][:, fi, xs:xs + qw],
                                                    start=(fi == 0), stop=(fi == 7))
                                return inst
                            tr.op("T", em_mm, reads=["Wbuf", xk],
                                  writes=[f"pb{bank}"])
                            def em_ev(E, ft=ft, bank=bank, qd=qd, qw=qw):
                                return E.activation(out_sb[:, ft, qd:qd + qw],
                                                    pb[bank][:, 0:qw], AF.Identity,
                                                    bias=bias_col[:, ft:ft + 1])
                            tr.op("S", em_ev, reads=[f"pb{bank}", bias_key],
                                  writes=[okey])
                            def em_sq(E, ft=ft, qd=qd, qw=qw):
                                return E.activation(q2buf[:, ft, 0:qw],
                                                    out_sb[:, ft, qd:qd + qw],
                                                    AF.Square)
                            tr.op("S", em_sq, reads=[okey], writes=["q2buf"])
                        def em_s1(E, qd=qd, qw=qw):
                            inst = None
                            for ft in range(8):
                                inst = E.matmul(pb[2][0:16, 0:qw], E16[:, ft, :],
                                                out_sb[:, ft, qd:qd + qw],
                                                start=(ft == 0), stop=(ft == 7))
                            return inst
                        tr.op("T", em_s1, reads=["E16", okey], writes=["pb2"])
                        def em_s2(E, qw=qw):
                            inst = None
                            for ft in range(8):
                                inst = E.matmul(pb[3][0:16, 0:qw], E16[:, ft, :],
                                                q2buf[:, ft, 0:qw],
                                                start=(ft == 0), stop=(ft == 7))
                            return inst
                        tr.op("T", em_s2, reads=["E16", "q2buf"], writes=["pb3"])

                    def lnchain(idx):
                        ci, xs, qd, qw = chunks[idx]
                        par = idx % 2
                        def em_ea(E, qw=qw):
                            return E.tensor_scalar(sta[:, 0:qw], pb[2][0:16, 0:qw],
                                                   1.0 / 64, None, ALU.mult)
                        tr.op("V", em_ea, reads=["pb2"], writes=["sta"])
                        def em_eb(E, qw=qw):
                            return E.tensor_scalar(stbx[:, 0:qw], pb[3][0:16, 0:qw],
                                                   1.0 / 64, None, ALU.mult)
                        tr.op("V", em_eb, reads=["pb3"], writes=["stbx"])
                        def em_m2(E, qw=qw):
                            return E.tensor_tensor(m2b[:, 0:qw], sta[:, 0:qw],
                                                   sta[:, 0:qw], ALU.mult)
                        tr.op("V", em_m2, reads=["sta"], writes=["m2b"])
                        def em_var(E, qw=qw):
                            return E.scalar_tensor_tensor(stbx[:, 0:qw], stbx[:, 0:qw],
                                                          EPS, m2b[:, 0:qw], ALU.add,
                                                          ALU.subtract)
                        tr.op("V", em_var, reads=["stbx", "m2b"], writes=["stbx"])
                        def em_ln(E, qw=qw):
                            return E.activation(stbx[:, 0:qw], stbx[:, 0:qw], AF.Ln)
                        tr.op("S", em_ln, reads=["stbx"], writes=["stbx"])
                        def em_ex2(E, qw=qw):
                            return E.activation(stbx[:, 0:qw], stbx[:, 0:qw], AF.Exp,
                                                scale=-0.5)
                        tr.op("S", em_ex2, reads=["stbx"], writes=["stbx"])
                        def em_rb(E, qw=qw, par=par):
                            return E.copy(rstd_b[par][:, 0:qw], stbx[:, 0:qw])
                        tr.op("S", em_rb, reads=["stbx"], writes=[f"rstd{par}"])
                        def em_mrs(E, qw=qw):
                            return E.tensor_tensor(m2b[:, 0:qw], sta[:, 0:qw],
                                                   stbx[:, 0:qw], ALU.mult)
                        tr.op("V", em_mrs, reads=["sta", "stbx"], writes=["m2b"])
                        def em_nmrs(E, qw=qw, par=par):
                            return E.tensor_scalar(rhsB_b[par][0:16, 0:qw],
                                                   m2b[:, 0:qw], -1.0, None, ALU.mult)
                        tr.op("V", em_nmrs, reads=["m2b"], writes=[f"rhsBm{par}"])

                    def apply_(idx):
                        ci, xs, qd, qw = chunks[idx]
                        par = idx % 2
                        for ft in range(8):
                            bp = (4, 5) if ft % 2 == 0 else (2, 3)
                            def em_a(E, ft=ft, AL=AL, qw=qw, par=par, bp=bp):
                                return E.matmul(pb[bp[0]][:, 0:qw], AL[:, ft, :],
                                                rstd_b[par][:, 0:qw], start=True,
                                                stop=True)
                            tr.op("T", em_a, reads=[f"AL{mat}", f"rstd{par}"],
                                  writes=[f"pb{bp[0]}"])
                            def em_b(E, ft=ft, BL=BL, qw=qw, par=par, bp=bp):
                                return E.matmul(pb[bp[1]][:, 0:qw], BL[:, ft, :],
                                                rhsB_b[par][:, 0:qw], start=True,
                                                stop=True)
                            tr.op("T", em_b, reads=[f"BL{mat}", f"rhsBm{par}",
                                                    f"rhsBc{par}"],
                                  writes=[f"pb{bp[1]}"])
                            half = ft % 2
                            sk = f"scr{half}"
                            def em_t1(E, ft=ft, qd=qd, qw=qw, half=half, bp=bp):
                                return E.tensor_tensor(
                                    scr[:, half * 512:half * 512 + qw],
                                    out_sb[:, ft, qd:qd + qw],
                                    pb[bp[0]][:, 0:qw], ALU.mult)
                            tr.op("V", em_t1, reads=[okey, f"pb{bp[0]}"],
                                  writes=[sk])
                            def em_t2(E, ft=ft, qd=qd, qw=qw, half=half, bp=bp):
                                return E.tensor_tensor(
                                    out_sb[:, ft, qd:qd + qw],
                                    scr[:, half * 512:half * 512 + qw],
                                    pb[bp[1]][:, 0:qw], ALU.add)
                            tr.op("V", em_t2, reads=[sk, f"pb{bp[1]}"],
                                  writes=[okey])

                    for idx in range(len(chunks)):
                        phase1(idx)
                        lnchain(idx)
                        if idx > 0:
                            apply_(idx - 1)
                    apply_(len(chunks) - 1)

                qk_pass(wqT, Qb, "Qb", bqc, "bqc", "q", ALq, BLq, True, eighth_row)
                qk_pass(wkT, Kb, "Kb", bkc, "bkc", "k", ALk, BLk, False, ones_row)

                tr.op("P", lambda E: E.dma_start(
                    out=Wbuf[:], in_=wvT.rearrange("(a p) f -> p a f", p=128)),
                    writes=["Wbuf"], dma=True)
                xchunk_load(0, 0, "xb0v")
                for ci in range(len(CH5)):
                    c0, cw = CH5[ci]
                    pp = ci % 2
                    xk = f"xb{pp}v"
                    if ci + 1 < len(CH5):
                        xchunk_load(ci + 1, (ci + 1) % 2, f"xb{(ci + 1) % 2}v")
                    for tl in range(cw // 128):
                        tt = c0 // 128 + tl
                        for fh in range(2):
                            bank = (0, 1, 6, 7)[(tt * 2 + fh) % 4]
                            def em_vmm(E, tl=tl, fh=fh, bank=bank, pp=pp):
                                inst = None
                                for fi in range(8):
                                    inst = E.matmul(pb[bank][:],
                                                    xbuf[pp][:, fi, tl * 128:(tl + 1) * 128],
                                                    Wbuf[:, fi, fh * 512:(fh + 1) * 512],
                                                    start=(fi == 0), stop=(fi == 7))
                                return inst
                            tr.op("T", em_vmm, reads=["Wbuf", xk], writes=[f"pb{bank}"])
                            def em_vev(E, tt=tt, fh=fh, bank=bank):
                                return E.scalar_tensor_tensor(
                                    Vb[:, tt, fh * 512:(fh + 1) * 512], pb[bank][:], 1.0,
                                    bvrep[:, fh * 512:(fh + 1) * 512], ALU.mult, ALU.add)
                            tr.op("V", em_vev, reads=[f"pb{bank}", "bvrep"], writes=["Vb"])
                tr.barrier()

            with ExitStack() as st1b:
                sb1b = mksb(st1b)
                Wobuf = sb1b("Wobuf", [128, 8, H], BF16)
                afm_t = sb1b("afm_t", [128, 8, 128], BF16)
                hwork = sb1b("hwork", [128, H])
                h32sb = sb1b("h32sb", [128, H])
                hbfsb = sb1b("hbfsb", [128, H], BF16)
                hfm = sb1b("hfm", [128, 8, 128])
                Pb2 = sb1b("Pb2", [128, 2, 384], BF16)
                PTs2 = sb1b("PTs2", [128, 2, 384], BF16)
                Dg2 = sb1b("Dg2", [128, 2, 128], BF16)
                nmax2 = sb1b("nmax2", [128, 2])
                den2 = sb1b("den2", [128, 2])
                rden2 = sb1b("rden2", [128, 2])
                tot16 = sb1b("tot16", [1, NTILE, 64])
                rnmax = sb1b("rnmax", [128, 1])
                tr.op("P", lambda E: E.dma_start(
                    out=Wobuf[:], in_=woT.rearrange("(a p) f -> p a f", p=128)),
                    writes=["Wo"], dma=True)
                def em_xr0(E):
                    return E.dma_start(out=xres[:, 0, :], in_=x_tm[WIN: WIN + 128, :])
                tr.op("Y", em_xr0, writes=["xres0"], dma=True)
                tr.op("V", lambda E: E.memset(lprefx[0:1, :], 0.0), writes=["lprefx"])

                SBK3 = (0, 3, 6)
                TBK = (1, 7)

                class _Defer:
                    def __init__(self):
                        self.q = []

                    def op(self, eng, emit, reads=(), writes=(), dma=False):
                        self.q.append((eng, emit, list(reads), list(writes), dma))

                pend = _Defer()

                def flush(n):
                    for _ in range(min(n, len(pend.q))):
                        eng, emit, rds, wrs, dmaf = pend.q.pop(0)
                        tr.op(eng, emit, reads=rds, writes=wrs, dma=dmaf)

                def stage_a(i0, j, h):
                    sbank = SBK3[i0 % 3]
                    mj = min(j, 2)
                    def em_s(E, j=j, h=h, sbank=sbank):
                        ft, s = h // 2, h % 2
                        return E.matmul(pb[sbank][:, 0:384],
                                        Qb[64 * s:64 * s + 64, ft,
                                           j * 128:(j + 1) * 128],
                                        Kb[64 * s:64 * s + 64, ft,
                                           j * 128:j * 128 + 384],
                                        start=True, stop=False)
                    tr.op("T", em_s, reads=["Qb", "Kb"], writes=[f"pb{sbank}"])
                    def em_msk(E, mj=mj, sbank=sbank):
                        return E.matmul(pb[sbank][:, 0:384], identb[:], masks[:, mj, :],
                                        start=False, stop=True)
                    tr.op("T", em_msk, reads=["identb", f"masks{mj}"],
                          writes=[f"pb{sbank}"])

                def stage_b1(i0, j, h):
                    sbank = SBK3[i0 % 3]
                    par = i0 % 2
                    def em_rm(E, sbank=sbank, par=par):
                        return E.reduce_max(nmax2[:, par:par + 1], pb[sbank][:, 0:384],
                                            axis=AX.X, negate=True)
                    tr.op("V", em_rm, reads=[f"pb{sbank}"], writes=[f"nmax{par}"])

                def stage_b2(i0, j, h):
                    sbank = SBK3[i0 % 3]
                    par = i0 % 2
                    def em_exp(E, sbank=sbank, par=par):
                        return E.activation(Pb2[:, par, :], pb[sbank][:, 0:384], AF.Exp,
                                            bias=nmax2[:, par:par + 1],
                                            accum_out=den2[:, par:par + 1])
                    tr.op("S", em_exp, reads=[f"pb{sbank}", f"nmax{par}"],
                          writes=[f"Pb{par}", f"den{par}"])
                    def em_rc(E, par=par):
                        return E.reciprocal(rden2[:, par:par + 1], den2[:, par:par + 1])
                    tr.op("V", em_rc, reads=[f"den{par}"], writes=[f"rden{par}"])
                    def em_dg(E, par=par):
                        return E.tensor_scalar(Dg2[:, par, :], identb[:],
                                               rden2[:, par:par + 1], None, ALU.mult)
                    tr.op("V", em_dg, reads=["identb", f"rden{par}"],
                          writes=[f"Dg{par}"])

                def stage_c1(i0, j, h):
                    par = i0 % 2
                    tbank = TBK[par]
                    def em_tp(E, par=par, tbank=tbank):
                        inst = None
                        for kk in range(3):
                            inst = E.matmul(pb[tbank][:, kk * 128:(kk + 1) * 128],
                                            Pb2[:, par, kk * 128:(kk + 1) * 128],
                                            Dg2[:, par, :], start=True, stop=True)
                        return inst
                    tr.op("T", em_tp, reads=[f"Pb{par}", f"Dg{par}"],
                          writes=[f"pb{tbank}"])
                    eng_e = "V" if par == 0 else "S"
                    def em_pte(E, par=par, tbank=tbank, eng_e=eng_e):
                        if eng_e == "V":
                            return E.tensor_copy(PTs2[:, par, :], pb[tbank][:, 0:384])
                        return E.copy(PTs2[:, par, :], pb[tbank][:, 0:384])
                    tr.op(eng_e, em_pte, reads=[f"pb{tbank}"], writes=[f"PTs{par}"])

                def stage_c2(i0, j, h):
                    par = i0 % 2
                    ft, s = h // 2, h % 2
                    reg = ft % 4
                    def em_pv(E, j=j, h=h, s=s, par=par, reg=reg):
                        inst = None
                        for kk in range(3):
                            inst = E.matmul(
                                pb[2][64 * s:64 * s + 64, reg * 128:(reg + 1) * 128],
                                Vb[:, j + kk, h * 64:(h + 1) * 64],
                                PTs2[:, par, kk * 128:(kk + 1) * 128],
                                start=(kk == 0), stop=(kk == 2),
                                tile_position=(0, 64 * s))
                        return inst
                    tr.op("T", em_pv, reads=["Vb", f"PTs{par}"], writes=[f"pb2r{reg}"])
                    if s == 1:
                        eng2 = "V" if ft % 2 == 0 else "S"
                        def em_pve(E, ft=ft, reg=reg, eng2=eng2):
                            if eng2 == "V":
                                return E.tensor_copy(
                                    afm_t[:, ft, :],
                                    pb[2][:, reg * 128:(reg + 1) * 128])
                            return E.copy(afm_t[:, ft, :],
                                          pb[2][:, reg * 128:(reg + 1) * 128])
                        tr.op(eng2, em_pve, reads=[f"pb2r{reg}"], writes=["afm"])

                def tail(j, T):
                    sl = j % 3
                    for fh in range(2):
                        bank = 4 + fh
                        def em_wo(E, fh=fh, bank=bank):
                            for kt in range(8):
                                E.matmul(pb[bank][:], afm_t[:, kt, :],
                                         Wobuf[:, kt, fh * 512:(fh + 1) * 512],
                                         start=(kt == 0), stop=False)
                            return E.matmul(pb[bank][:], onesrow_f[0:1, 0:128],
                                            borep[0:1, fh * 512:(fh + 1) * 512],
                                            start=False, stop=True)
                        T.op("T", em_wo, reads=["afm", "Wo", "borep", "onesrow_f"],
                             writes=[f"pb{bank}"] + (["pb5b", "pb5t"] if bank == 5
                                                     else []))
                    for fh in range(2):
                        bank = 4 + fh
                        def em_h1(E, fh=fh, bank=bank, sl=sl):
                            return E.scalar_tensor_tensor(
                                hwork[:, fh * 512:(fh + 1) * 512], pb[bank][:], 1.0,
                                xres[:, sl, fh * 512:(fh + 1) * 512],
                                ALU.mult, ALU.add)
                        T.op("V", em_h1, reads=[f"pb{bank}", f"xres{sl}"],
                             writes=["hwork"])
                    T.op("S", lambda E: E.activation(scr[:], hwork[:], AF.Identity,
                                                     accum_out=lnred[:, 0:1]),
                         reads=["hwork"], writes=["scr0", "scr1", "lnm"])
                    T.op("S", lambda E: E.activation(scr[:], hwork[:], AF.Square,
                                                     accum_out=lnred[:, 1:2]),
                         reads=["hwork"], writes=["scr0", "scr1", "lnv"])
                    T.op("V", lambda E: E.tensor_scalar(lnred[:, 0:1], lnred[:, 0:1],
                                                        1.0 / H, None, ALU.mult),
                         reads=["lnm"], writes=["lnm"])
                    T.op("V", lambda E: E.tensor_tensor(lnred[:, 3:4], lnred[:, 0:1],
                                                        lnred[:, 0:1], ALU.mult),
                         reads=["lnm"], writes=["lnt"])
                    T.op("V", lambda E: E.tensor_scalar(lnred[:, 1:2], lnred[:, 1:2],
                                                        1.0 / H, EPS, ALU.mult,
                                                        ALU.add),
                         reads=["lnv"], writes=["lnv"])
                    T.op("V", lambda E: E.tensor_tensor(lnred[:, 1:2], lnred[:, 1:2],
                                                        lnred[:, 3:4], ALU.subtract),
                         reads=["lnv", "lnt"], writes=["lnv"])
                    T.op("S", lambda E: E.activation(lnred[:, 2:3], lnred[:, 1:2],
                                                     AF.Ln),
                         reads=["lnv"], writes=["rstdh"])
                    T.op("S", lambda E: E.activation(lnred[:, 2:3], lnred[:, 2:3],
                                                     AF.Exp, scale=-0.5),
                         reads=["rstdh"], writes=["rstdh"])
                    T.op("V", lambda E: E.tensor_tensor(lnred[:, 3:4], lnred[:, 0:1],
                                                        lnred[:, 2:3], ALU.mult),
                         reads=["lnm", "rstdh"], writes=["lnt"])
                    T.op("V", lambda E: E.tensor_scalar(h32sb[:], hwork[:],
                                                        lnred[:, 2:3], lnred[:, 3:4],
                                                        ALU.mult, ALU.subtract),
                         reads=["hwork", "rstdh", "lnt"], writes=["h32sb"])
                    T.op("S", lambda E: E.copy(hbfsb[:], h32sb[:]),
                         reads=["h32sb"], writes=["hbfsb"])
                    def em_dh32(E, j=j):
                        return E.dma_start(out=h32_d[j * 128:(j + 1) * 128, :],
                                           in_=h32sb[:])
                    T.op("Y", em_dh32, reads=["h32sb"], writes=[f"h32d{j}"], dma=True)
                    def em_dhbf(E, j=j):
                        return E.dma_start(out=hbf_in[j * 128:(j + 1) * 128, :],
                                           in_=hbfsb[:])
                    T.op("S", em_dhbf, reads=["hbfsb"], writes=[f"hbfd{j}"], dma=True)
                    if j % 4 == 3 and j != NTILE - 1:
                        ch = j // 4
                        def em_agh(E, ch=ch):
                            return E.collective_compute(
                                "AllGather", ALU.bypass, replica_groups=RG,
                                ins=[hbf_in[ch * 512:(ch + 1) * 512, :]],
                                outs=[hall[ch * 4096:(ch + 1) * 4096, :]])
                        T.op("P", em_agh,
                             reads=[f"hbfd{jj}" for jj in range(ch * 4, ch * 4 + 4)],
                             writes=[f"hall_d{ch}"])
                    for kt in range(8):
                        tb = 4 if kt % 2 == 0 else 5
                        def em_ht(E, kt=kt, tb=tb):
                            return E.transpose(pb[tb][:, 0:128],
                                               h32sb[:, kt * 128:(kt + 1) * 128],
                                               identf[:])
                        T.op("T", em_ht, reads=["h32sb", "identf"],
                             writes=[f"pb{tb}"] + (["pb5b", "pb5t"] if tb == 5
                                                   else []))
                        enghe = "V" if kt % 2 == 0 else "S"
                        def em_he(E, kt=kt, tb=tb, enghe=enghe):
                            if enghe == "V":
                                return E.tensor_copy(hfm[:, kt, :], pb[tb][:, 0:128])
                            return E.copy(hfm[:, kt, :], pb[tb][:, 0:128])
                        T.op(enghe, em_he, reads=[f"pb{tb}"], writes=["hfm"])
                    def em_lg(E):
                        inst = None
                        for kt in range(8):
                            inst = E.matmul(pb[4][:, 0:64], hfm[:, kt, :],
                                            gtsb[:, kt, :],
                                            start=(kt == 0), stop=(kt == 7))
                        return inst
                    T.op("T", em_lg, reads=["hfm", "gtsb"], writes=["pb4"])
                    def em_le(E, j=j):
                        return E.tensor_copy(logits[:, j, :], pb[4][:, 0:64])
                    T.op("V", em_le, reads=["pb4"], writes=[f"lg{j}"])
                    def em_nm(E, j=j):
                        return E.reduce_max(rnmax[:, 0:1], logits[:, j, :], axis=AX.X,
                                            negate=True)
                    T.op("V", em_nm, reads=[f"lg{j}"], writes=["rnmax"])
                    def em_ex(E, j=j):
                        return E.activation(Esb[:], logits[:, j, :], AF.Exp,
                                            bias=rnmax[:, 0:1])
                    T.op("S", em_ex, reads=[f"lg{j}", "rnmax"], writes=["Esb"])
                    T.op("V", lambda E: E.max(w8v[:], Esb[:]), reads=["Esb"],
                         writes=["w8v"])
                    T.op("V", lambda E: E.max_index(idx8u[:], w8v[:], Esb[:]),
                         reads=["w8v", "Esb"], writes=["idx8u"])
                    def em_i8(E, j=j):
                        return E.tensor_copy(idx8a[:, j, :], idx8u[:])
                    T.op("V", em_i8, reads=["idx8u"], writes=["idx8a"])
                    T.op("V", lambda E: E.tensor_copy(Ec[:], Esb[:]), reads=["Esb"],
                         writes=["Ec"])
                    T.op("V", lambda E: E.match_replace(Ec[:], w8v[:], Ec[:], -1.0e30),
                         reads=["w8v", "Ec"], writes=["Ec"])
                    T.op("V", lambda E: E.tensor_scalar(Rb[:], Ec[:], -1.0e29, None,
                                                        ALU.is_le),
                         reads=["Ec"], writes=["Rb"])
                    T.op("V", lambda E: E.reduce_sum(ws[:, 0:1], w8v[:], axis=AX.X),
                         reads=["w8v"], writes=["ws"])
                    T.op("V", lambda E: E.reciprocal(rws[:, 0:1], ws[:, 0:1]),
                         reads=["ws"], writes=["rws"])
                    def em_w8(E, j=j):
                        return E.tensor_scalar(w8na[:, j, :], w8v[:], rws[:, 0:1],
                                               None, ALU.mult)
                    T.op("V", em_w8, reads=["w8v", "rws"], writes=["w8na"])
                    def em_pe(E):
                        return E.matmul(pb[5][:, 0:64], trib[:], Rb[:], start=True,
                                        stop=False)
                    T.op("T", em_pe, reads=["trib", "Rb"], writes=["pb5"])
                    def em_lb(E):
                        return E.matmul(pb[5][:, 0:64], onesrow_f[0:1, 0:128],
                                        lprefx[0:1, 0:64], start=False, stop=True)
                    T.op("T", em_lb, reads=["lprefx", "onesrow_f"], writes=["pb5"])
                    def em_tt(E):
                        return E.matmul(pb[5][0:1, 64:128], onescol_b[:], Rb[:],
                                        start=True, stop=True)
                    T.op("T", em_tt, reads=["onescol_b", "Rb"], writes=["pb5t"])
                    def em_te(E, j=j):
                        return E.tensor_copy(tot16[0:1, j, :], pb[5][0:1, 64:128])
                    T.op("V", em_te, reads=["pb5t"], writes=["tot16"])
                    T.op("V", lambda E: E.tensor_copy(posg[:], pb[5][:, 0:64]),
                         reads=["pb5"], writes=["posg"])
                    def em_if(E, j=j):
                        return E.tensor_copy(idx8f[:], idx8a[:, j, :])
                    T.op("V", em_if, reads=["idx8a"], writes=["idx8f"])
                    for k in range(8):
                        def em_pk(E, j=j, k=k):
                            return E.scalar_tensor_tensor(
                                junk64[:], iota64r[:], idx8f[:, k:k + 1], posg[:],
                                ALU.is_equal, ALU.mult,
                                accum_out=lposk8[:, j, k:k + 1])
                        T.op("V", em_pk, reads=["iota64r", "idx8f", "posg"],
                             writes=["junk64", "lposk8"])
                    def em_v8(E, j=j):
                        return E.tensor_scalar(valid8[:], lposk8[:, j, :], CAP - 0.5,
                                               None, ALU.is_lt)
                    T.op("V", em_v8, reads=["lposk8"], writes=["valid8"])
                    T.op("V", lambda E: E.tensor_scalar(idx256[:], idx8f[:], 256.0,
                                                        None, ALU.mult),
                         reads=["idx8f"], writes=["idx256"])
                    pp = j % 4
                    def em_p0(E, j=j, pp=pp):
                        return E.tensor_copy(
                            pays[pp][:, :, 0:1],
                            tids1[:, j:j + 1].unsqueeze(1).broadcast_to([128, 8, 1]))
                    T.op("V", em_p0, reads=["tids1"], writes=[f"pay{pp}"])
                    def em_p1(E, j=j, pp=pp):
                        return E.tensor_copy(pays[pp][:, :, 1], w8na[:, j, :])
                    T.op("V", em_p1, reads=["w8na"], writes=[f"pay{pp}"])
                    def em_s0(E, j=j):
                        return E.tensor_tensor(soff8[:], idx256[:], lposk8[:, j, :],
                                               ALU.add)
                    T.op("V", em_s0, reads=["idx256", "lposk8"], writes=["soff8"])
                    T.op("V", lambda E: E.tensor_tensor(soff8[:], soff8[:], valid8[:],
                                                        ALU.mult),
                         reads=["soff8", "valid8"], writes=["soff8"])
                    T.op("V", lambda E: E.tensor_scalar(tmp8[:], valid8[:], -1.0, 1.0,
                                                        ALU.mult, ALU.add),
                         reads=["valid8"], writes=["tmp8"])
                    T.op("V", lambda E: E.tensor_scalar(tmp8[:], tmp8[:],
                                                        trashcol[:, 0:1], None,
                                                        ALU.mult),
                         reads=["tmp8", "trashcol"], writes=["tmp8"])
                    T.op("V", lambda E: E.tensor_tensor(soff8[:], soff8[:], tmp8[:],
                                                        ALU.add),
                         reads=["soff8", "tmp8"], writes=["soff8"])
                    def em_su(E, pp=pp):
                        return E.tensor_copy(soffu[pp][:], soff8[:])
                    T.op("V", em_su, reads=["soff8"], writes=[f"soffu{pp}"])
                    for k in range(8):
                        def em_sc(E, k=k, pp=pp, j=j):
                            return E.indirect_dma_start(
                                out=ts_in,
                                out_offset=IndirectOffsetOnAxis(
                                    ap=soffu[pp][:, k:k + 1], axis=0),
                                in_=pays[pp][:, k, :], in_offset=None)
                        T.op("P", em_sc, reads=[f"soffu{pp}", f"pay{pp}", "tsz"],
                             writes=[f"sc{j}_{k}"], dma=True)
                    def em_lu(E, j=j):
                        return E.tensor_tensor(lprefx[0:1, :], lprefx[0:1, :],
                                               tot16[0:1, j, :], ALU.add)
                    T.op("V", em_lu, reads=["tot16", "lprefx"], writes=["lprefx"])

                steps = [(j, h) for j in range(NTILE) for h in range(NH)]
                NSTEP = len(steps)
                for i in range(NSTEP + 4):
                    if i < NSTEP:
                        j, h = steps[i]
                        if h == 0 and j + 1 < NTILE:
                            def em_xr(E, j=j):
                                return E.dma_start(
                                    out=xres[:, (j + 1) % 3, :],
                                    in_=x_tm[WIN + (j + 1) * 128:
                                             WIN + (j + 2) * 128, :])
                            tr.op("Y", em_xr, writes=[f"xres{(j + 1) % 3}"], dma=True)
                        stage_a(i, j, h)
                    if 1 <= i < NSTEP + 1:
                        stage_b1(i - 1, *steps[i - 1])
                    if 2 <= i < NSTEP + 2:
                        stage_b2(i - 2, *steps[i - 2])
                    if 3 <= i < NSTEP + 3:
                        stage_c1(i - 3, *steps[i - 3])
                    if 4 <= i < NSTEP + 4:
                        j0, h0 = steps[i - 4]
                        stage_c2(i - 4, j0, h0)
                        if h0 == NH - 1:
                            tail(j0, pend)
                    flush(5)
                flush(len(pend.q))
                tr.op("V", lambda E: E.tensor_copy(ctot[0:1, :], lprefx[0:1, :]),
                      reads=["lprefx"], writes=["ctot"])
                tr.op("S", lambda E: E.dma_start(out=tot_in, in_=ctot[:]),
                      reads=["ctot"], writes=["totin_d"], dma=True)
                tr.op("P", lambda E: E.collective_compute(
                    "AllGather", ALU.bypass, replica_groups=RG, ins=[tot_in],
                    outs=[tot_out]),
                    reads=["totin_d"], writes=["totout_d"])
                def em_agh3(E):
                    return E.collective_compute(
                        "AllGather", ALU.bypass, replica_groups=RG,
                        ins=[hbf_in[3 * 512:4 * 512, :]],
                        outs=[hall[3 * 4096:4 * 4096, :]])
                tr.op("P", em_agh3, reads=[f"hbfd{jj}" for jj in range(12, 16)],
                      writes=["hall_d3"])

                tr.barrier()
            tr.barrier()

        tr.op("Y", lambda E: E.dma_start(out=totall[:], in_=tot_out), reads=["totout_d"],
              writes=["totall"], dma=True)
        tr.op("T", lambda E: E.matmul(pb[2][0:1, 0:64], cmsk[:], totall[:],
                                      start=True, stop=True),
              reads=["cmsk", "totall"], writes=["pb2"])
        tr.op("V", lambda E: E.tensor_copy(coreoff[:], pb[2][0:1, 0:64]),
              reads=["pb2"], writes=["coreoff"])

        with ExitStack() as st4:
            sb4 = mksb(st4)
            Dtm = sb4("Dtm", [128, 16, H], BF16)
            Dfm = [sb4(f"Dfm{i}", [128, 8, 256], BF16) for i in range(2)]
            Hfm = sb4("Hfm", [128, 6, 256], BF16)
            gsil = sb4("gsil", [128, 2, 256])
            Ysb = sb4("Ysb", [128, 2, 512], BF16)
            wg_sb = [sb4(f"wg{i}", [128, 8, INTER], BF16) for i in range(2)]
            wu_sb = [sb4(f"wu{i}", [128, 8, INTER], BF16) for i in range(2)]
            wd_sb = [sb4(f"wd{i}", [128, 6, H], BF16) for i in range(2)]
            zrow = sb4("zrow", [1, H], BF16)
            coreoffb = sb4("coreoffb", [128, 64])
            agT = sb4("agT", [128, 16], U32)
            lm9sb = sb4("lm9sb", [8, 9])
            sel64sb = sb4("sel64sb", [64, 8])
            CTsb = sb4("CTsb", [64, 9])
            gposf = sb4("gposf", [64, 256])
            e256f = sb4("e256f", [64, 256])
            rcA = sb4("rcA", [64, 256]); rcB = sb4("rcB", [64, 256])
            rsA = sb4("rsA", [64, 256]); rsB = sb4("rsB", [64, 256])
            rtq = sb4("rtq", [64, 256]); remp = sb4("remp", [64, 256])
            ragw = sb4("ragw", [64, 256])

            def load_ew(e, slot):
                def em_g(E, e=e, slot=slot):
                    return E.dma_start(out=wg_sb[slot][:],
                                       in_=wgT[e].rearrange("(a p) f -> p a f", p=128))
                tr.op("P", em_g, writes=[f"wg{slot}"], dma=True)
                def em_u(E, e=e, slot=slot):
                    return E.dma_start(out=wu_sb[slot][:],
                                       in_=wuT[e].rearrange("(a p) f -> p a f", p=128))
                tr.op("P", em_u, writes=[f"wu{slot}"], dma=True)
                def em_d(E, e=e, slot=slot):
                    return E.dma_start(out=wd_sb[slot][:],
                                       in_=wdT[e].rearrange("(a p) f -> p a f", p=128))
                tr.op("P", em_d, writes=[f"wd{slot}"], dma=True)

            tr.op("P", lambda E: E.collective_compute(
                "AllGather", ALU.bypass, replica_groups=RG, ins=[ts_in], outs=[ts_all]),
                reads=[f"sc{j}_{k}" for j in range(NTILE) for k in range(8)] + ["tsz"],
                writes=["tsall_d"])
            load_ew(0, 0)
            load_ew(1, 1)

            bcast_row(coreoffb, coreoff, 128, 64, "coreoff", "coreoffb")
            tr.op("Y", lambda E: E.dma_start(out=lm9sb[:], in_=lmask9), writes=["lm9sb"],
                  dma=True)
            tr.op("Y", lambda E: E.dma_start(out=sel64sb[:], in_=sel64), writes=["sel64sb"],
                  dma=True)
            tr.op("T", lambda E: E.matmul(pb[3][0:64, 0:9], totall[:], lm9sb[:],
                                          start=True, stop=True),
                  reads=["totall", "lm9sb"], writes=["pb3"])
            tr.op("V", lambda E: E.tensor_copy(CTsb[:], pb[3][0:64, 0:9]),
                  reads=["pb3"], writes=["CTsb"])
            tr.op("P", lambda E: E.iota(gposf[:], [[1, 256]], base=0, channel_multiplier=0,
                                        allow_small_or_imprecise_dtypes=True),
                  writes=["gposf"])
            tr.op("P", lambda E: E.iota(e256f[:], [[0, 256]], base=0,
                                        channel_multiplier=256,
                                        allow_small_or_imprecise_dtypes=True),
                  writes=["e256f"])
            rcb = {"rcA": rcA, "rcB": rcB}
            rsb_m = {"rsA": rsA, "rsB": rsB}
            tr.op("V", lambda E: E.tensor_scalar(rcA[:], gposf[:], CTsb[:, 1:2], None,
                                                 ALU.is_ge),
                  reads=["gposf", "CTsb"], writes=["rcA"])
            rc_cur = "rcA"
            for cp in range(2, 9):
                nxt = "rcB" if rc_cur == "rcA" else "rcA"
                def em_rc(E, cp=cp, rc_cur=rc_cur, nxt=nxt):
                    return E.scalar_tensor_tensor(rcb[nxt][:], gposf[:],
                                                  CTsb[:, cp:cp + 1], rcb[rc_cur][:],
                                                  ALU.is_ge, ALU.add)
                tr.op("V", em_rc, reads=["gposf", "CTsb", rc_cur], writes=[nxt])
                rc_cur = nxt
            tr.op("V", lambda E: E.memset(rsA[:], 0.0), writes=["rsA"])
            rs_cur = "rsA"
            for cp in range(1, 9):
                def em_oq(E, cp=cp, rc_cur=rc_cur):
                    return E.tensor_scalar(rtq[:], rcb[rc_cur][:], float(cp), None,
                                           ALU.is_equal)
                tr.op("V", em_oq, reads=[rc_cur], writes=["rtq"])
                nxt = "rsB" if rs_cur == "rsA" else "rsA"
                def em_sq(E, cp=cp, rs_cur=rs_cur, nxt=nxt):
                    return E.scalar_tensor_tensor(rsb_m[nxt][:], rtq[:],
                                                  CTsb[:, cp:cp + 1], rsb_m[rs_cur][:],
                                                  ALU.mult, ALU.add)
                tr.op("V", em_sq, reads=["rtq", "CTsb", rs_cur], writes=[nxt])
                rs_cur = nxt
            def em_g1(E, rs_cur=rs_cur):
                return E.tensor_tensor(ragw[:], gposf[:], rsb_m[rs_cur][:], ALU.subtract)
            tr.op("V", em_g1, reads=["gposf", rs_cur], writes=["ragw"])
            tr.op("V", lambda E: E.tensor_tensor(ragw[:], ragw[:], e256f[:], ALU.add),
                  reads=["ragw", "e256f"], writes=["ragw"])
            def em_g2(E, rc_cur=rc_cur):
                return E.scalar_tensor_tensor(ragw[:], rcb[rc_cur][:], float(TSZ), ragw[:],
                                              ALU.mult, ALU.add)
            tr.op("V", em_g2, reads=[rc_cur, "ragw"], writes=["ragw"])
            tr.op("V", lambda E: E.tensor_scalar(remp[:], gposf[:], CTsb[:, 8:9], None,
                                                 ALU.is_ge),
                  reads=["gposf", "CTsb"], writes=["remp"])
            tr.op("V", lambda E: E.tensor_scalar(rtq[:], remp[:], -1.0, 1.0, ALU.mult,
                                                 ALU.add),
                  reads=["remp"], writes=["rtq"])
            tr.op("V", lambda E: E.tensor_tensor(ragw[:], ragw[:], rtq[:], ALU.mult),
                  reads=["ragw", "rtq"], writes=["ragw"])
            tr.op("V", lambda E: E.tensor_scalar(rtq[:], remp[:], float(TALL + 128), None,
                                                 ALU.mult),
                  reads=["remp"], writes=["rtq"])
            tr.op("V", lambda E: E.tensor_tensor(ragw[:], ragw[:], rtq[:], ALU.add),
                  reads=["ragw", "rtq"], writes=["ragw"])
            for hi in range(2):
                bk = 2 if hi == 0 else 3
                def em_sel(E, hi=hi, bk=bk):
                    return E.matmul(pb[bk][:, 0:8], ragw[:, hi * 128:(hi + 1) * 128],
                                    sel64sb[:], start=True, stop=True)
                tr.op("T", em_sel, reads=["ragw", "sel64sb"], writes=[f"pb{bk}"])
                def em_agt(E, hi=hi, bk=bk):
                    return E.tensor_copy(agT[:, hi * 8:(hi + 1) * 8], pb[bk][:, 0:8])
                tr.op("V", em_agt, reads=[f"pb{bk}"], writes=["agT"])
            for g in range(16):
                col = (g % 2) * 8 + g // 2
                def em_twg(E, g=g, col=col):
                    return E.indirect_dma_start(
                        out=tw_my[:, g, :], out_offset=None, in_=ts_all,
                        in_offset=IndirectOffsetOnAxis(ap=agT[:, col:col + 1], axis=0))
                tr.op("P", em_twg, reads=["agT", "tsall_d"], writes=["tw_my"], dma=True)
            tr.op("V", lambda E: E.tensor_scalar(hmsk[:], tw_my[:, :, 0], 0.5,
                                                 float(TALL + 1), ALU.is_le, ALU.mult),
                  reads=["tw_my"], writes=["hmsk"])
            tr.op("V", lambda E: E.tensor_scalar(hidxf[:], tw_my[:, :, 0], -1.0, None,
                                                 ALU.add),
                  reads=["tw_my"], writes=["hidxf"])
            tr.op("V", lambda E: E.tensor_tensor(hidxf[:], hidxf[:], hmsk[:], ALU.add),
                  reads=["hidxf", "hmsk"], writes=["hidxf"])
            tr.op("V", lambda E: E.tensor_copy(hidxu[:], hidxf[:]),
                  reads=["hidxf"], writes=["hidxu"])

            tr.op("V", lambda E: E.memset(zrow[:], 0.0), writes=["zrow"])
            tr.op("P", lambda E: E.dma_start(out=hall[TALL:TALL + 1, :], in_=zrow[:]),
                  reads=["zrow"], writes=["hallz"], dma=True)
            for g in range(16):
                def em_dg2(E, g=g):
                    return E.indirect_dma_start(
                        out=Dtm[:, g, :], out_offset=None, in_=hall,
                        in_offset=IndirectOffsetOnAxis(ap=hidxu[:, g:g + 1], axis=0))
                tr.op("P", em_dg2,
                      reads=["hidxu", "hallz"] + [f"hall_d{c}" for c in range(4)],
                      writes=[f"Dtm{g}"], dma=True)

            for e in range(EPL):
                slot = e % 2
                for ct in range(2):
                    g = 2 * e + ct
                    for ftt in range(8):
                        tb = 0 if ftt % 2 == 0 else 1
                        def em_dt(E, g=g, ftt=ftt, tb=tb):
                            return E.transpose(pbf(tb, 128),
                                               Dtm[:, g, ftt * 128:(ftt + 1) * 128],
                                               identb[:])
                        tr.op("T", em_dt, reads=[f"Dtm{g}", "identb"], writes=[f"pb{tb}"])
                        eng = "V" if ftt % 2 == 0 else "S"
                        def em_de(E, ct=ct, ftt=ftt, tb=tb, slot=slot, eng=eng):
                            if eng == "V":
                                return E.tensor_copy(
                                    Dfm[slot][:, ftt, ct * 128:(ct + 1) * 128], pbf(tb, 128))
                            return E.copy(Dfm[slot][:, ftt, ct * 128:(ct + 1) * 128],
                                          pbf(tb, 128))
                        tr.op(eng, em_de, reads=[f"pb{tb}"], writes=[f"Dfm{slot}"])
                for it in range(6):
                    gb = (2, 6)[it % 2]
                    ub = (3, 7)[it % 2]
                    def em_g(E, it=it, slot=slot, gb=gb):
                        inst = None
                        for ftt in range(8):
                            inst = E.matmul(pb[gb][:, 0:256],
                                            wg_sb[slot][:, ftt, it * 128:(it + 1) * 128],
                                            Dfm[slot][:, ftt, :],
                                            start=(ftt == 0), stop=(ftt == 7))
                        return inst
                    tr.op("T", em_g, reads=[f"wg{slot}", f"Dfm{slot}"],
                          writes=[f"pb{gb}"])
                    def em_u(E, it=it, slot=slot, ub=ub):
                        inst = None
                        for ftt in range(8):
                            inst = E.matmul(pb[ub][:, 0:256],
                                            wu_sb[slot][:, ftt, it * 128:(it + 1) * 128],
                                            Dfm[slot][:, ftt, :],
                                            start=(ftt == 0), stop=(ftt == 7))
                        return inst
                    tr.op("T", em_u, reads=[f"wu{slot}", f"Dfm{slot}"],
                          writes=[f"pb{ub}"])
                    def em_sl(E, it=it, gb=gb):
                        return E.activation(gsil[:, it % 2, :], pb[gb][:, 0:256],
                                            AF.Silu)
                    tr.op("S", em_sl, reads=[f"pb{gb}"], writes=[f"gsil{it % 2}"])
                    def em_hp(E, it=it, ub=ub):
                        return E.tensor_tensor(Hfm[:, it, :], gsil[:, it % 2, :],
                                               pb[ub][:, 0:256], ALU.mult)
                    tr.op("V", em_hp, reads=[f"gsil{it % 2}", f"pb{ub}"],
                          writes=["Hfm"])
                for ct in range(2):
                    g = 2 * e + ct
                    for fh in range(2):
                        bank = 4 + fh
                        def em_dn(E, ct=ct, fh=fh, slot=slot, bank=bank):
                            inst = None
                            for it in range(6):
                                inst = E.matmul(pb[bank][:],
                                                Hfm[:, it, ct * 128:(ct + 1) * 128],
                                                wd_sb[slot][:, it, fh * 512:(fh + 1) * 512],
                                                start=(it == 0), stop=(it == 5))
                            return inst
                        tr.op("T", em_dn, reads=[f"wd{slot}", "Hfm"], writes=[f"pb{bank}"])
                        def em_ev(E, g=g, fh=fh, bank=bank):
                            return E.tensor_scalar(Ysb[:, fh, :], pb[bank][:],
                                                   tw_my[:, g, 1:2], None, ALU.mult)
                        tr.op("V", em_ev, reads=[f"pb{bank}", "tw_my"], writes=[f"Ysb{fh}"])
                        def em_st(E, g=g, fh=fh):
                            return E.dma_start(out=yexp_in[g * 128:(g + 1) * 128,
                                                           fh * 512:(fh + 1) * 512],
                                               in_=Ysb[:, fh, :])
                        tr.op("S", em_st, reads=[f"Ysb{fh}"], writes=[f"yst{g}_{fh}"],
                              dma=True)
                if e + 2 < EPL:
                    load_ew(e + 2, e % 2)
                for jt in (2 * e, 2 * e + 1):
                    def em_if2(E, j=jt):
                        return E.tensor_copy(idx8f[:], idx8a[:, j, :])
                    tr.op("V", em_if2, reads=["idx8a"], writes=["idx8f"])
                    for k in range(8):
                        def em_oh2(E, k=k):
                            return E.tensor_scalar(ohk[:], iota64r[:], idx8f[:, k:k + 1],
                                                   None, ALU.is_equal)
                        tr.op("V", em_oh2, reads=["iota64r", "idx8f"], writes=["ohk"])
                        def em_ck(E, k=k):
                            return E.scalar_tensor_tensor(junk64[:], coreoffb[:], 1.0,
                                                          ohk[:], ALU.mult, ALU.mult,
                                                          accum_out=posk8[:, k:k + 1])
                        tr.op("V", em_ck, reads=["coreoffb", "ohk"],
                              writes=["junk64", "posk8"])
                        def em_ek2(E, k=k):
                            return E.scalar_tensor_tensor(junk64[:], erowb[:], 1.0,
                                                          ohk[:], ALU.mult, ALU.mult,
                                                          accum_out=erowk8[:, k:k + 1])
                        tr.op("V", em_ek2, reads=["erowb", "ohk"],
                              writes=["junk64", "erowk8"])
                    def em_gp(E, j=jt):
                        return E.tensor_tensor(posk8[:], posk8[:], lposk8[:, j, :],
                                               ALU.add)
                    tr.op("V", em_gp, reads=["posk8", "lposk8"], writes=["posk8"])
                    tr.op("V", lambda E: E.tensor_scalar(valid8[:], posk8[:], CAP - 0.5,
                                                         None, ALU.is_lt),
                          reads=["posk8"], writes=["valid8"])
                    tr.op("V", lambda E: E.tensor_scalar(coff8[:], posk8[:], CAP - 1.0,
                                                         None, ALU.min),
                          reads=["posk8"], writes=["coff8"])
                    def em_ce(E, j=jt):
                        return E.tensor_tensor(coff8[:], coff8[:], erowk8[:], ALU.add)
                    tr.op("V", em_ce, reads=["coff8", "erowk8"], writes=["coff8"])
                    tr.op("V", lambda E: E.tensor_tensor(coff8[:], coff8[:], valid8[:],
                                                         ALU.mult),
                          reads=["coff8", "valid8"], writes=["coff8"])
                    tr.op("V", lambda E: E.tensor_scalar(tmp8[:], valid8[:], -float(TALL),
                                                         float(TALL), ALU.mult, ALU.add),
                          reads=["valid8"], writes=["tmp8"])
                    tr.op("V", lambda E: E.tensor_tensor(coff8[:], coff8[:], tmp8[:],
                                                         ALU.add),
                          reads=["coff8", "tmp8"], writes=["coff8"])
                    def em_cu2(E, j=jt):
                        return E.tensor_copy(offs_all[:, j, :], coff8[:])
                    tr.op("V", em_cu2, reads=["coff8"], writes=[f"offs{jt}"])
                if e % 4 == 3:
                    yc = e // 4
                    def em_agy(E, yc=yc):
                        return E.collective_compute(
                            "AllGather", ALU.bypass, replica_groups=RG,
                            ins=[yexp_in[yc * 1024:(yc + 1) * 1024, :]],
                            outs=[yall[yc * 8192:(yc + 1) * 8192, :]])
                    tr.op("P", em_agy,
                          reads=[f"yst{g}_{fh}" for g in range(yc * 8, yc * 8 + 8)
                                 for fh in range(2)],
                          writes=[f"yall_d{yc}"])

            tr.op("P", lambda E: E.dma_start(out=yall[TALL:TALL + 1, :], in_=zrow[:]),
                  reads=["zrow"], writes=["yzpad"], dma=True)
            tr.barrier()

        with ExitStack() as st5:
            sb5 = mksb(st5)
            gdp = [sb5(f"gdp{i}", [128, 8, H]) for i in range(2)]
            s_a = sb5("s_a", [128, H]); s_c = sb5("s_c", [128, H])
            s_b = sb5("s_b", [128, H]); s_d = sb5("s_d", [128, H])
            outw = sb5("outw", [128, H])
            def em_hr0(E):
                return E.dma_start(out=xres[:, 0, :], in_=h32_d[0:128, :])
            tr.op("Y", em_hr0, reads=["h32d0"], writes=["xres0"], dma=True)
            for j in range(NTILE):
                sl = j % 2
                if j + 1 < NTILE:
                    def em_hr(E, j=j):
                        return E.dma_start(out=xres[:, (j + 1) % 2, :],
                                           in_=h32_d[(j + 1) * 128:(j + 2) * 128, :])
                    tr.op("Y", em_hr, reads=[f"h32d{j + 1}"],
                          writes=[f"xres{(j + 1) % 2}"], dma=True)
                for kk in range(8):
                    def em_g0(E, j=j, kk=kk, sl=sl):
                        return E.indirect_dma_start(
                            out=gdp[sl][:, kk, :], out_offset=None, in_=yall,
                            in_offset=IndirectOffsetOnAxis(
                                ap=offs_all[:, j, kk:kk + 1], axis=0))
                    tr.op("P", em_g0,
                          reads=[f"offs{j}", "yall_d0", "yall_d1", "yzpad"],
                          writes=[f"gdp{sl}_{kk}"], dma=True)
                for dst, key, a, bq_ in ((s_a, "s_a", 0, 1), (s_b, "s_b", 2, 3),
                                         (s_c, "s_c", 4, 5), (s_d, "s_d", 6, 7)):
                    def em_s1(E, sl=sl, dst=dst, a=a, bq_=bq_):
                        return E.tensor_tensor(dst[:], gdp[sl][:, a, :],
                                               gdp[sl][:, bq_, :], ALU.add)
                    rds = ([f"gdp{sl}_{t}" for t in range(8)] if a == 0
                           else [f"gdp{sl}_{a}", f"gdp{sl}_{bq_}"])
                    tr.op("V", em_s1, reads=rds, writes=[key])
                tr.op("V", lambda E: E.tensor_tensor(s_a[:], s_a[:], s_b[:], ALU.add),
                      reads=["s_a", "s_b"], writes=["s_a"])
                tr.op("V", lambda E: E.tensor_tensor(s_c[:], s_c[:], s_d[:], ALU.add),
                      reads=["s_c", "s_d"], writes=["s_c"])
                tr.op("V", lambda E: E.tensor_tensor(s_a[:], s_a[:], s_c[:], ALU.add),
                      reads=["s_a", "s_c"], writes=["s_a"])
                def em_res(E, sl=sl):
                    return E.tensor_tensor(outw[:], s_a[:], xres[:, sl, :], ALU.add)
                tr.op("V", em_res, reads=["s_a", f"xres{sl}"], writes=["outw"])
                tr.op("V", lambda E: E.reduce_sum(lnred[:, 0:1], outw[:], axis=AX.X),
                      reads=["outw"], writes=["lnm"])
                tr.op("V", lambda E: E.tensor_scalar(lnred[:, 0:1], lnred[:, 0:1], 1.0 / H,
                                                     None, ALU.mult),
                      reads=["lnm"], writes=["lnm"])
                tr.op("V", lambda E: E.tensor_scalar(outw[:], outw[:], lnred[:, 0:1], None,
                                                     ALU.subtract),
                      reads=["outw", "lnm"], writes=["outw"])
                tr.op("S", lambda E: E.activation(scr[:], outw[:], AF.Square,
                                                  accum_out=lnred[:, 1:2]),
                      reads=["outw"], writes=["scr0", "scr1", "lnv"])
                tr.op("V", lambda E: E.tensor_scalar(lnred[:, 1:2], lnred[:, 1:2], 1.0 / H,
                                                     EPS, ALU.mult, ALU.add),
                      reads=["lnv"], writes=["lnv"])
                tr.op("S", lambda E: E.activation(lnred[:, 2:3], lnred[:, 1:2], AF.Ln),
                      reads=["lnv"], writes=["rstdh"])
                tr.op("S", lambda E: E.activation(lnred[:, 2:3], lnred[:, 2:3], AF.Exp,
                                                  scale=-0.5),
                      reads=["rstdh"], writes=["rstdh"])
                tr.op("V", lambda E: E.tensor_scalar(outw[:], outw[:], lnred[:, 2:3], None,
                                                     ALU.mult),
                      reads=["outw", "rstdh"], writes=["outw"])
                def em_yo(E, j=j):
                    return E.dma_start(out=y_out[j * 128:(j + 1) * 128, :], in_=outw[:])
                tr.op("S", em_yo, reads=["outw"], writes=[f"yo{j}"], dma=True)

            tr.wait_all_dma()
            tr.build(block)
    return nc


_NC_CACHE = None


def kernel(**inputs):
    global _NC_CACHE
    x = np.asarray(inputs["L0_attn_x"], np.float32)
    Wq = np.ascontiguousarray(np.asarray(inputs["L0_attn_Wq"], np.float32).T)
    Wk = np.ascontiguousarray(np.asarray(inputs["L0_attn_Wk"], np.float32).T)
    Wv = np.ascontiguousarray(np.asarray(inputs["L0_attn_Wv"], np.float32).T)
    Wo = np.ascontiguousarray(np.asarray(inputs["L0_attn_Wo"], np.float32).T)
    bq = np.asarray(inputs["L0_attn_bq"], np.float32)
    bk = np.asarray(inputs["L0_attn_bk"], np.float32)
    bv = np.asarray(inputs["L0_attn_bv"], np.float32)
    bo = np.asarray(inputs["L0_attn_bo"], np.float32)
    lnqw = np.asarray(inputs["L0_attn_ln_q_w"], np.float32)
    lnqb = np.asarray(inputs["L0_attn_ln_q_b"], np.float32)
    lnkw = np.asarray(inputs["L0_attn_ln_k_w"], np.float32)
    lnkb = np.asarray(inputs["L0_attn_ln_k_b"], np.float32)
    gate = np.asarray(inputs["L0_ffn_gate_w"], np.float32)
    wgate = np.asarray(inputs["L0_ffn_w_gate"], np.float32)
    wup = np.asarray(inputs["L0_ffn_w_up"], np.float32)
    wdown = np.asarray(inputs["L0_ffn_w_down"], np.float32)

    B, S, _ = x.shape
    gateT = np.ascontiguousarray(gate.T)
    wgTa = np.ascontiguousarray(wgate.transpose(0, 2, 1))
    wuTa = np.ascontiguousarray(wup.transpose(0, 2, 1))
    wdTa = np.ascontiguousarray(wdown.transpose(0, 2, 1))
    tri = np.triu(np.ones((128, 128), np.float32), 1)
    ident = np.eye(128, dtype=np.float32)
    ones_row = np.ones((1, 512), np.float32)
    eighth_row = np.full((1, 512), 0.125, np.float32)
    ee = np.arange(NEXP)
    erow_np = ((ee % 8 // 4) * 8192 + (ee // 8) * 1024 +
               (ee % 8 % 4) * 256).astype(np.float32).reshape(1, NEXP)
    bq_fm = np.ascontiguousarray(bq.reshape(8, 128).T)
    bk_fm = np.ascontiguousarray(bk.reshape(8, 128).T)

    in_maps = []
    for c in range(NCORES):
        b, half = c // 2, c % 2
        lo = half * TCORE - WIN
        if lo < 0:
            xc = np.concatenate([np.zeros((WIN, H), np.float32),
                                 x[b, 0:half * TCORE + TCORE]], axis=0)
        else:
            xc = x[b, lo:half * TCORE + TCORE]
        xc = np.ascontiguousarray(xc)
        tvals_c = np.array([[256.0, 128.0]] if half == 0 else [[0.0, 0.0]], np.float32)
        t0 = c * TCORE
        jj = np.arange(NTILE)[None, :]
        tids_c = ((jj // 4) * 4096 + c * 512 + (jj % 4) * 128 +
                  np.arange(128)[:, None]).astype(np.float32)
        cm = (np.arange(8) < c).astype(np.float32).reshape(8, 1)
        in_maps.append({
            "x_tm": xc,
            "x_fmT": np.ascontiguousarray(xc.T),
            "wqT": Wq, "wkT": Wk, "wvT": Wv, "woT": Wo,
            "bq_fm": bq_fm, "bk_fm": bk_fm,
            "bv_row": bv.reshape(1, H), "bo_row": bo.reshape(1, H),
            "lnqw2": np.ascontiguousarray(np.tile(lnqw, 2).reshape(1, 128)),
            "lnqb2": np.ascontiguousarray(np.tile(lnqb, 2).reshape(1, 128)),
            "lnkw2": np.ascontiguousarray(np.tile(lnkw, 2).reshape(1, 128)),
            "lnkb2": np.ascontiguousarray(np.tile(lnkb, 2).reshape(1, 128)),
            "gateT": gateT,
            "wgT": np.ascontiguousarray(wgTa[c * EPL:(c + 1) * EPL]),
            "wuT": np.ascontiguousarray(wuTa[c * EPL:(c + 1) * EPL]),
            "wdT": np.ascontiguousarray(wdTa[c * EPL:(c + 1) * EPL]),
            "tri_in": tri, "ident_in": ident,
            "tvals": tvals_c, "tids": tids_c, "coremask": cm,
            "lmask9": np.triu(np.ones((8, 9), np.float32), 1),
            "sel64": np.ascontiguousarray(
                np.eye(64, dtype=np.float32)[:, c * 8:(c + 1) * 8]),
            "ones_row": ones_row, "eighth_row": eighth_row, "erow_in": erow_np,
        })
    if _NC_CACHE is None:
        _NC_CACHE = build_program()
    res = run_bass_kernel_spmd(_NC_CACHE, in_maps, core_ids=list(range(NCORES)))
    out = np.empty((B, S, H), np.float32)
    for c in range(NCORES):
        b, half = c // 2, c % 2
        out[b, half * TCORE:(half + 1) * TCORE] = res.results[c]["y"]
    return out


# version 21
# speedup vs baseline: 1.0167x; 1.0167x over previous
import numpy as np
import ml_dtypes
import concourse.bass as bass
import concourse.mybir as mybir
from concourse.bass_core_rust import IndirectOffsetOnAxis
from concourse.bass_utils import run_bass_kernel_spmd

F32 = mybir.dt.float32
BF16 = mybir.dt.bfloat16
U32 = mybir.dt.uint32
AF = mybir.ActivationFunctionType
ALU = mybir.AluOpType
AX = mybir.AxisListType

NCORES = 8
H = 1024
NH, HD = 16, 64
WIN = 256
TCORE = 2048
TK = TCORE + WIN
NTILE = TCORE // 128
NKT = TK // 128
NEXP, TOPK, CAP, INTER = 64, 8, 256, 768
EPL = NEXP // NCORES
SLOTS = EPL * CAP
TALL = NCORES * TCORE
EPS = 1e-5
BIG = 1.0e6

ENGS = ("T", "V", "S", "P", "Y")


class Tracker:

    def __init__(self, nc):
        self.nc = nc
        self._sem_cm = {e: nc.alloc_semaphore(f"cm_{e}") for e in ENGS}
        self._sem_dma = {e: nc.alloc_semaphore(f"dma_{e}") for e in ENGS}
        self.cnt = {e: 0 for e in ENGS}
        self.dcnt = {e: 0 for e in ENGS}
        self.last_w = {}
        self.readers = {}
        self.wm = {}
        self.ops = {e: [] for e in ENGS}

    def _sem(self, kind, e):
        return self._sem_cm[e] if kind == "c" else self._sem_dma[e]

    def _need(self, waiter, kind, e, val):
        if val <= 0:
            return None
        if kind == "c" and e == waiter == "T":
            return None
        k = (waiter, (kind, e))
        if self.wm.get(k, 0) >= val:
            return None
        self.wm[k] = val
        return (kind, e, val)

    def op(self, eng, emit, reads=(), writes=(), dma=False):
        waits = []
        for k in reads:
            w = self.last_w.get(k)
            if w is not None:
                n = self._need(eng, w[0], w[1], w[2])
                if n:
                    waits.append(n)
        for k in writes:
            w = self.last_w.get(k)
            if w is not None:
                n = self._need(eng, w[0], w[1], w[2])
                if n:
                    waits.append(n)
            for (kind, e), val in self.readers.get(k, {}).items():
                n = self._need(eng, kind, e, val)
                if n:
                    waits.append(n)
        if dma:
            self.dcnt[eng] += 1
            ev = ("d", eng, self.dcnt[eng] * 16)
        else:
            self.cnt[eng] += 1
            ev = ("c", eng, self.cnt[eng])
        for k in writes:
            self.last_w[k] = ev
            self.readers[k] = {}
        for k in reads:
            self.readers.setdefault(k, {})[(ev[0], eng)] = ev[2]
        self.ops[eng].append((waits, emit, ev))
        return ev

    def barrier(self, skip=None):
        skip = skip or {}
        snap = [("c", e, self.cnt[e] - skip.get(e, 0)) for e in ENGS] + \
               [("d", e, self.dcnt[e] * 16) for e in ENGS]
        for eng in ENGS:
            waits = []
            for kind, e, val in snap:
                n = self._need(eng, kind, e, val)
                if n:
                    waits.append(n)
            if waits:
                self.ops[eng].append((waits, None, None))

    def wait_all_dma(self):
        for eng in ENGS:
            n = self._need(eng, "d", eng, self.dcnt[eng] * 16)
            if n:
                self.ops[eng].append(([n], None, None))

    def build(self, block):
        tr = self

        def mk(e_name):
            def run(engine):
                for waits, emit, ev in tr.ops[e_name]:
                    for kind, pe, val in waits:
                        engine.wait_ge(tr._sem(kind, pe), val)
                    if emit is None:
                        continue
                    inst = emit(engine)
                    if ev is not None:
                        if ev[0] == "d":
                            inst.then_inc(tr._sem_dma[e_name], 16)
                        else:
                            inst.then_inc(tr._sem_cm[e_name], 1)
            return run

        block.tensor(mk("T"))
        block.vector(mk("V"))
        block.scalar(mk("S"))
        block.gpsimd(mk("P"))
        block.sync(mk("Y"))


def build_program():
    nc = bass.Bass("TRN2", target_bir_lowering=False, num_devices=NCORES)
    RG = [list(range(NCORES))]

    def din(name, shape, dt=F32):
        return nc.dram_tensor(name, shape, dt, kind="ExternalInput").ap()

    x_tm = din("x_tm", [TK, H])
    x_fmT = din("x_fmT", [H, TK])
    wqT = din("wqT", [H, H]); wkT = din("wkT", [H, H])
    wvT = din("wvT", [H, H]); woT = din("woT", [H, H])
    bq_fm = din("bq_fm", [128, 8]); bk_fm = din("bk_fm", [128, 8])
    bv_row = din("bv_row", [1, H]); bo_row = din("bo_row", [1, H])
    lnqw2 = din("lnqw2", [1, 128]); lnqb2 = din("lnqb2", [1, 128])
    lnkw2 = din("lnkw2", [1, 128]); lnkb2 = din("lnkb2", [1, 128])
    gateT = din("gateT", [H, NEXP])
    wgT = din("wgT", [EPL, H, INTER])
    wuT = din("wuT", [EPL, H, INTER])
    wdT = din("wdT", [EPL, INTER, H])
    tri_in = din("tri_in", [128, 128])
    ident_in = din("ident_in", [128, 128])
    tvals = din("tvals", [1, 2])
    tids = din("tids", [128, NTILE])
    coremask = din("coremask", [8, 1])
    lmask9 = din("lmask9", [8, 9])
    sel64 = din("sel64", [64, 8])
    ones_row = din("ones_row", [1, 512])
    erow_in = din("erow_in", [1, NEXP])
    eighth_row = din("eighth_row", [1, 512])

    y_out = nc.dram_tensor("y", [TCORE, H], F32, kind="ExternalOutput").ap()

    h32_d = nc.dram_tensor("h32_d", [TCORE, H], F32, kind="Internal").ap()
    hbf_in = nc.dram_tensor("hbf_in", [TCORE, H], BF16, kind="Internal").ap()
    hall = nc.dram_tensor("hall", [TALL + 128, H], BF16, kind="Internal",
                          addr_space="Shared").ap()
    tot_in = nc.dram_tensor("tot_in", [1, NEXP], F32, kind="Internal").ap()
    tot_out = nc.dram_tensor("tot_out", [NCORES, NEXP], F32, kind="Internal",
                             addr_space="Shared").ap()
    TSZ = TALL + 256
    ts_in = nc.dram_tensor("ts_in", [TSZ, 2], F32, kind="Internal").ap()
    ts_all = nc.dram_tensor("ts_all", [8 * TSZ, 2], F32, kind="Internal",
                            addr_space="Shared").ap()
    yexp_in = nc.dram_tensor("yexp_in", [SLOTS, H], BF16, kind="Internal").ap()
    yall = nc.dram_tensor("yall", [TALL + 128, H], BF16, kind="Internal",
                          addr_space="Shared").ap()

    tr = Tracker(nc)
    from contextlib import ExitStack
    with ExitStack() as st:
        def mksb(stk):
            def sbf(name, shape, dt=F32):
                return stk.enter_context(nc.sbuf_tensor(name, shape, dt))
            return sbf
        sb = mksb(st)

        identb = sb("identb", [128, 128], BF16)
        identf = sb("identf", [128, 128])
        trib = sb("trib", [128, 128], BF16)
        masks = sb("masks", [128, 3, 384], BF16)
        bqc = sb("bqc", [128, 8]); bkc = sb("bkc", [128, 8])
        bvrep = sb("bvrep", [128, H]); borep = sb("borep", [128, H])
        gtsb = sb("gtsb", [128, 8, NEXP])
        ALq = sb("ALq", [16, 8, 128], BF16); BLq = sb("BLq", [17, 8, 128], BF16)
        ALk = sb("ALk", [16, 8, 128], BF16); BLk = sb("BLk", [17, 8, 128], BF16)
        E16 = sb("E16", [128, 8, 16], BF16)
        iota64r = sb("iota64r", [128, 64])
        iota16r = sb("iota16r", [128, 16])
        tidsb = sb("tidsb", [128, NTILE])
        tids1 = sb("tids1", [128, NTILE])
        cmsk = sb("cmsk", [8, 1])
        onesrow_f = sb("onesrow_f", [1, 128])
        onescol_b = sb("onescol_b", [128, 1], BF16)
        selcol = sb("selcol", [128, 1])
        sel8col = sb("sel8col", [128, 1])
        trashcol = sb("trashcol", [128, 1])
        erowb = sb("erowb", [128, NEXP])
        erowk8 = sb("erowk8", [128, 8])
        logits = sb("logits", [128, NTILE, 64])
        idx8a = sb("idx8a", [128, NTILE, 8])
        w8na = sb("w8na", [128, NTILE, 8])
        lposk8 = sb("lposk8", [128, NTILE, 8])
        lprefx = sb("lprefx", [1, 64])
        coreoff = sb("coreoff", [1, 64])
        offs_all = sb("offs_all", [128, NTILE, 8], U32)
        tw_my = sb("tw_my", [128, 16, 2])
        hidxu = sb("hidxu", [128, 16], U32)
        scr = sb("scr", [128, H])
        lnred = sb("lnred", [128, 4])
        xres = sb("xres", [128, 3, H])
        nmax = sb("nmax", [128, 1]); den = sb("den", [128, 1]); rden = sb("rden", [128, 1])
        Esb = sb("Esb", [128, 64]); Ec = sb("Ec", [128, 64])
        w8v = sb("w8v", [128, 8]); idx8u = sb("idx8u", [128, 8], U32)
        Rb = sb("Rb", [128, 64], BF16)
        ws = sb("ws", [128, 1]); rws = sb("rws", [128, 1])
        ctot = sb("ctot", [1, 64]); totall = sb("totall", [8, 64])
        posg = sb("posg", [128, 64]); ohk = sb("ohk", [128, 64])
        junk64 = sb("junk64", [128, 64])
        posk8 = sb("posk8", [128, 8]); valid8 = sb("valid8", [128, 8])
        idx256 = sb("idx256", [128, 8]); soff8 = sb("soff8", [128, 8])
        coff8 = sb("coff8", [128, 8]); tmp8 = sb("tmp8", [128, 8])
        pays = [sb(f"pays{i}", [128, 8, 2]) for i in range(4)]
        soffu = [sb(f"soffu{i}", [128, 8], U32) for i in range(4)]
        idx8f = sb("idx8f", [128, 8])
        hidxf = sb("hidxf", [128, 16]); hmsk = sb("hmsk", [128, 16])

        pb = [st.enter_context(nc.psum_tensor(f"pb{i}", [128, 512], F32))
              for i in range(8)]

        def pbf(i, cols):
            return pb[i][:].bitcast(BF16)[:, 0:cols]

        def bcast_row(dst, src_ap, M, N, rkey, wkey):
            for c0 in range(0, N, 512):
                cw = min(512, N - c0)
                def em_bm(E, c0=c0, cw=cw, M=M):
                    return E.matmul(pb[7][0:M, 0:cw], onesrow_f[0:1, 0:M],
                                    src_ap[0:1, c0:c0 + cw], start=True, stop=True)
                tr.op("T", em_bm, reads=[rkey, "onesrow_f"], writes=["pb7"])
                def em_bc(E, c0=c0, cw=cw, M=M, dst=dst):
                    return E.tensor_copy(dst[0:M, c0:c0 + cw], pb[7][0:M, 0:cw])
                tr.op("V", em_bc, reads=["pb7"], writes=[wkey])

        block = st.enter_context(nc.Block())

        tr.op("P", lambda E: E.dma_start(out=identb[:], in_=ident_in), writes=["identb"], dma=True)
        tr.op("V", lambda E: E.memset(onesrow_f[:], 1.0), writes=["onesrow_f"])
        tr.op("V", lambda E: E.memset(onescol_b[:], 1.0), writes=["onescol_b"])
        tr.op("Y", lambda E: E.dma_start(out=identf[:], in_=ident_in), writes=["identf"], dma=True)
        tr.op("P", lambda E: E.dma_start(out=trib[:], in_=tri_in), writes=["trib"], dma=True)
        tr.op("Y", lambda E: E.dma_start(out=bqc[:], in_=bq_fm), writes=["bqc"], dma=True)
        tr.op("Y", lambda E: E.dma_start(out=bkc[:], in_=bk_fm), writes=["bkc"], dma=True)
        tr.op("Y", lambda E: E.dma_start(out=bvrep[0:1, :], in_=bv_row), writes=["bvrep"], dma=True)
        tr.op("Y", lambda E: E.dma_start(out=borep[0:1, :], in_=bo_row), writes=["borep"], dma=True)
        bcast_row(bvrep, bvrep, 128, H, "bvrep", "bvrep")
        bcast_row(borep, borep, 128, H, "borep", "borep")
        tr.op("Y", lambda E: E.dma_start(out=gtsb[:], in_=gateT.rearrange("(a p) e -> p a e", p=128)),
              writes=["gtsb"], dma=True)
        tr.op("Y", lambda E: E.dma_start(out=tidsb[:], in_=tids), writes=["tidsb"], dma=True)
        tr.op("V", lambda E: E.tensor_scalar(tids1[:], tidsb[:], 1.0, None, ALU.add),
              reads=["tidsb"], writes=["tids1"])
        tr.op("Y", lambda E: E.dma_start(out=cmsk[:], in_=coremask), writes=["cmsk"], dma=True)
        tr.op("P", lambda E: E.iota(iota64r[:], [[1, 64]], base=0, channel_multiplier=0,
                                    allow_small_or_imprecise_dtypes=True), writes=["iota64r"])
        tr.op("P", lambda E: E.iota(iota16r[:], [[1, 16]], base=0, channel_multiplier=0,
                                    allow_small_or_imprecise_dtypes=True), writes=["iota16r"])
        tr.op("V", lambda E: E.memset(scr[:, 0:260], 0.0), writes=["scr0"])
        tr.op("P", lambda E: E.dma_start(
            out=ts_in.rearrange("(p a) c -> p (a c)", p=128), in_=scr[:, 0:260]),
            reads=["scr0"], writes=["tsz"], dma=True)
        tr.op("P", lambda E: E.iota(selcol[:], [[1, 1]], base=0, channel_multiplier=1,
                                    allow_small_or_imprecise_dtypes=True), writes=["selcol"])
        tr.op("V", lambda E: E.tensor_scalar(selcol[:], selcol[:], 63.5, None, ALU.is_gt),
              reads=["selcol"], writes=["selcol"])
        tr.op("V", lambda E: E.tensor_scalar(sel8col[:], selcol[:], 8.0, None, ALU.mult),
              reads=["selcol"], writes=["sel8col"])
        tr.op("P", lambda E: E.iota(trashcol[:], [[1, 1]], base=TALL, channel_multiplier=1,
                                    allow_small_or_imprecise_dtypes=True), writes=["trashcol"])
        tr.op("Y", lambda E: E.dma_start(out=erowb[0:1, :], in_=erow_in), writes=["erowb"], dma=True)
        bcast_row(erowb, erowb, 128, NEXP, "erowb", "erowb")
        for ft in range(8):
            def em_e16(E, ft=ft):
                return E.tensor_scalar(E16[:, ft, :], iota16r[:], float(ft), sel8col[:, 0:1],
                                       ALU.subtract, ALU.is_equal)
            tr.op("V", em_e16, reads=["iota16r", "sel8col"], writes=["E16"])
        tr.op("S", lambda E: E.activation(lnred[0:1, 0:1], onesrow_f[0:1, 0:1], AF.Exp),
              reads=["onesrow_f"], writes=["lnred"])

        with ExitStack() as st0:
            sb0 = mksb(st0)
            mband = sb0("mband", [128, 384])
            mtmp = sb0("mtmp", [128, 384])
            miotc = sb0("miotc", [128, 384])
            tvsb = sb0("tvsb", [128, 2])
            tr.op("Y", lambda E: E.dma_start(out=tvsb[0:1, :], in_=tvals), writes=["tvsb"], dma=True)
            bcast_row(tvsb, tvsb, 128, 2, "tvsb", "tvsb")
            tr.op("P", lambda E: E.iota(mband[:], [[1, 384]], base=0, channel_multiplier=-1,
                                        allow_small_or_imprecise_dtypes=True), writes=["mband"])
            tr.op("P", lambda E: E.iota(miotc[:], [[1, 384]], base=0, channel_multiplier=0,
                                        allow_small_or_imprecise_dtypes=True), writes=["miotc"])
            tr.op("V", lambda E: E.tensor_scalar(mtmp[:], mband[:], -0.5, -1.0e30,
                                                 ALU.is_lt, ALU.mult),
                  reads=["mband"], writes=["mtmp"])
            tr.op("V", lambda E: E.tensor_scalar(mband[:], mband[:], WIN + 0.5, -1.0e30,
                                                 ALU.is_gt, ALU.mult),
                  reads=["mband"], writes=["mband"])
            tr.op("V", lambda E: E.tensor_tensor(masks[:, 2, :], mband[:], mtmp[:], ALU.add),
                  reads=["mband", "mtmp"], writes=["masks2"])
            for j in (0, 1):
                def em_sp(E, j=j):
                    return E.tensor_scalar(mtmp[:], miotc[:], tvsb[:, j:j + 1], -1.0e30,
                                           ALU.is_lt, ALU.mult)
                tr.op("V", em_sp, reads=["miotc", "tvsb"], writes=["mtmp"])
                def em_m(E, j=j):
                    return E.tensor_tensor(masks[:, j, :], masks[:, 2, :], mtmp[:], ALU.add)
                tr.op("V", em_m, reads=["masks2", "mtmp"], writes=[f"masks{j}"])

            g2q = sb0("g2q", [1, 128]); g2k = sb0("g2k", [1, 128])
            gb16 = sb0("gb16", [16, 128])
            tgt16 = sb0("tgt16", [16, 128])
            selb16 = sb0("selb16", [16, 128])
            iota16c = sb0("iota16c", [16, 1])
            selrow = sb0("selrow", [1, 128])
            tr.op("Y", lambda E: E.dma_start(out=g2q[:], in_=lnqw2), writes=["g2q"], dma=True)
            tr.op("Y", lambda E: E.dma_start(out=g2k[:], in_=lnkw2), writes=["g2k"], dma=True)
            tr.op("V", lambda E: E.tensor_scalar(g2q[:], g2q[:], 0.125, None, ALU.mult),
                  reads=["g2q"], writes=["g2q"])
            tr.op("P", lambda E: E.iota(iota16c[:], [[1, 1]], base=0, channel_multiplier=1,
                                        allow_small_or_imprecise_dtypes=True), writes=["iota16c"])
            tr.op("P", lambda E: E.iota(selrow[:], [[1, 128]], base=0, channel_multiplier=0,
                                        allow_small_or_imprecise_dtypes=True), writes=["selrow"])
            tr.op("V", lambda E: E.tensor_scalar(selrow[:], selrow[:], 63.5, 8.0,
                                                 ALU.is_gt, ALU.mult),
                  reads=["selrow"], writes=["selrow"])
            bcast_row(selb16, selrow, 16, 128, "selrow", "selb16")
            for mat, AL, BL, g2, gkey, beta_in in (
                    ("q", ALq, BLq, g2q, "g2q", lnqb2),
                    ("k", ALk, BLk, g2k, "g2k", lnkb2)):
                bcast_row(gb16, g2, 16, 128, gkey, "gb16")
                for ft in range(8):
                    def em_tg(E, ft=ft):
                        return E.tensor_scalar(tgt16[:], selb16[:], float(ft), None, ALU.add)
                    tr.op("V", em_tg, reads=["selb16"], writes=["tgt16"])
                    def em_eq(E):
                        return E.tensor_scalar(tgt16[:], tgt16[:], iota16c[:, 0:1], None,
                                               ALU.is_equal)
                    tr.op("V", em_eq, reads=["tgt16", "iota16c"], writes=["tgt16"])
                    def em_al(E, AL=AL, ft=ft):
                        return E.tensor_tensor(AL[:, ft, :], tgt16[:], gb16[:], ALU.mult)
                    tr.op("V", em_al, reads=["tgt16", "gb16"], writes=[f"AL{mat}"])
                    def em_bl(E, BL=BL, ft=ft):
                        return E.tensor_tensor(BL[0:16, ft, :], tgt16[:], gb16[:], ALU.mult)
                    tr.op("V", em_bl, reads=["tgt16", "gb16"], writes=[f"BL{mat}"])
                    def em_blb(E, BL=BL, ft=ft, beta_in=beta_in):
                        return E.dma_start(out=BL[16:17, ft, :], in_=beta_in)
                    tr.op("P", em_blb, writes=[f"BL{mat}"], dma=True)
            tr.barrier()

        with ExitStack() as st1:
            sb1 = mksb(st1)
            Qb = sb1("Qb", [128, 8, TCORE], BF16)
            Kb = sb1("Kb", [128, 8, TK], BF16)
            Vb = sb1("Vb", [128, NKT, H], BF16)

            CH5 = [(0, 512), (512, 512), (1024, 512), (1536, 512), (2048, 256)]

            with ExitStack() as st1a:
                sb1a = mksb(st1a)
                Wbuf = sb1a("Wbuf", [128, 8, H], BF16)
                xbuf = [sb1a(f"xbuf{i}", [128, 8, 512], BF16) for i in range(2)]
                q2buf = sb1a("q2buf", [128, 8, 512], BF16)
                sta = sb1a("sta", [16, 512])
                stbx = sb1a("stbx", [16, 512])
                m2b = sb1a("m2b", [16, 512])
                rstd_b = [sb1a(f"rstd_b{i}", [16, 512], BF16) for i in range(2)]
                rhsB_b = [sb1a(f"rhsB_b{i}", [17, 512], BF16) for i in range(2)]

                def xchunk_load(ci, pp, key):
                    c0, cw = CH5[ci]
                    def em_x(E, c0=c0, cw=cw, pp=pp):
                        return E.dma_start(
                            out=xbuf[pp][:, :, 0:cw],
                            in_=x_fmT.rearrange("(a p) t -> p a t", p=128)[:, :, c0:c0 + cw])
                    tr.op("P", em_x, writes=[key], dma=True)

                def qk_pass(wT_in, out_sb, okey, bias_col, bias_key, mat, AL, BL,
                            is_q, row16_const):
                    tr.op("P", lambda E, w=wT_in: E.dma_start(
                        out=Wbuf[:], in_=w.rearrange("(a p) f -> p a f", p=128)),
                        writes=["Wbuf"], dma=True)
                    for par in range(2):
                        def em_c16(E, rc=row16_const, par=par):
                            return E.dma_start(out=rhsB_b[par][16:17, :], in_=rc)
                        tr.op("P", em_c16, writes=[f"rhsBc{par}"], dma=True)
                    xchunk_load(0, 0, f"xb0{mat}")
                    B4 = (0, 1, 6, 7)
                    chunks = []
                    for ci in range(len(CH5)):
                        c0, cw = CH5[ci]
                        if is_q:
                            xs = WIN if ci == 0 else 0
                            qd = c0 + xs - WIN
                            qw = cw - xs
                        else:
                            xs, qd, qw = 0, c0, cw
                        if qw > 0:
                            chunks.append((ci, xs, qd, qw))

                    def phase1(idx):
                        ci, xs, qd, qw = chunks[idx]
                        pp = ci % 2
                        xk = f"xb{pp}{mat}"
                        if ci + 1 < len(CH5):
                            xchunk_load(ci + 1, (ci + 1) % 2,
                                        f"xb{(ci + 1) % 2}{mat}")
                        for ft in range(8):
                            bank = B4[ft % 4]
                            def em_mm(E, ft=ft, bank=bank, pp=pp, xs=xs, qw=qw):
                                inst = None
                                for fi in range(8):
                                    inst = E.matmul(pb[bank][:, 0:qw],
                                                    Wbuf[:, fi,
                                                         ft * 128:(ft + 1) * 128],
                                                    xbuf[pp][:, fi, xs:xs + qw],
                                                    start=(fi == 0), stop=(fi == 7))
                                return inst
                            tr.op("T", em_mm, reads=["Wbuf", xk],
                                  writes=[f"pb{bank}"])
                            def em_ev(E, ft=ft, bank=bank, qd=qd, qw=qw):
                                return E.activation(out_sb[:, ft, qd:qd + qw],
                                                    pb[bank][:, 0:qw], AF.Identity,
                                                    bias=bias_col[:, ft:ft + 1])
                            tr.op("S", em_ev, reads=[f"pb{bank}", bias_key],
                                  writes=[okey])
                            def em_sq(E, ft=ft, qd=qd, qw=qw):
                                return E.activation(q2buf[:, ft, 0:qw],
                                                    out_sb[:, ft, qd:qd + qw],
                                                    AF.Square)
                            tr.op("S", em_sq, reads=[okey], writes=["q2buf"])
                        def em_s1(E, qd=qd, qw=qw):
                            inst = None
                            for ft in range(8):
                                inst = E.matmul(pb[2][0:16, 0:qw], E16[:, ft, :],
                                                out_sb[:, ft, qd:qd + qw],
                                                start=(ft == 0), stop=(ft == 7))
                            return inst
                        tr.op("T", em_s1, reads=["E16", okey], writes=["pb2"])
                        def em_s2(E, qw=qw):
                            inst = None
                            for ft in range(8):
                                inst = E.matmul(pb[3][0:16, 0:qw], E16[:, ft, :],
                                                q2buf[:, ft, 0:qw],
                                                start=(ft == 0), stop=(ft == 7))
                            return inst
                        tr.op("T", em_s2, reads=["E16", "q2buf"], writes=["pb3"])

                    def lnchain(idx):
                        ci, xs, qd, qw = chunks[idx]
                        par = idx % 2
                        def em_ea(E, qw=qw):
                            return E.tensor_scalar(sta[:, 0:qw], pb[2][0:16, 0:qw],
                                                   1.0 / 64, None, ALU.mult)
                        tr.op("V", em_ea, reads=["pb2"], writes=["sta"])
                        def em_eb(E, qw=qw):
                            return E.tensor_scalar(stbx[:, 0:qw], pb[3][0:16, 0:qw],
                                                   1.0 / 64, None, ALU.mult)
                        tr.op("V", em_eb, reads=["pb3"], writes=["stbx"])
                        def em_m2(E, qw=qw):
                            return E.tensor_tensor(m2b[:, 0:qw], sta[:, 0:qw],
                                                   sta[:, 0:qw], ALU.mult)
                        tr.op("V", em_m2, reads=["sta"], writes=["m2b"])
                        def em_var(E, qw=qw):
                            return E.scalar_tensor_tensor(stbx[:, 0:qw], stbx[:, 0:qw],
                                                          EPS, m2b[:, 0:qw], ALU.add,
                                                          ALU.subtract)
                        tr.op("V", em_var, reads=["stbx", "m2b"], writes=["stbx"])
                        def em_ln(E, qw=qw):
                            return E.activation(stbx[:, 0:qw], stbx[:, 0:qw], AF.Ln)
                        tr.op("S", em_ln, reads=["stbx"], writes=["stbx"])
                        def em_ex2(E, qw=qw):
                            return E.activation(stbx[:, 0:qw], stbx[:, 0:qw], AF.Exp,
                                                scale=-0.5)
                        tr.op("S", em_ex2, reads=["stbx"], writes=["stbx"])
                        def em_rb(E, qw=qw, par=par):
                            return E.copy(rstd_b[par][:, 0:qw], stbx[:, 0:qw])
                        tr.op("S", em_rb, reads=["stbx"], writes=[f"rstd{par}"])
                        def em_mrs(E, qw=qw):
                            return E.tensor_tensor(m2b[:, 0:qw], sta[:, 0:qw],
                                                   stbx[:, 0:qw], ALU.mult)
                        tr.op("V", em_mrs, reads=["sta", "stbx"], writes=["m2b"])
                        def em_nmrs(E, qw=qw, par=par):
                            return E.tensor_scalar(rhsB_b[par][0:16, 0:qw],
                                                   m2b[:, 0:qw], -1.0, None, ALU.mult)
                        tr.op("V", em_nmrs, reads=["m2b"], writes=[f"rhsBm{par}"])

                    def apply_(idx):
                        ci, xs, qd, qw = chunks[idx]
                        par = idx % 2
                        for ft in range(8):
                            bp = (4, 5) if ft % 2 == 0 else (2, 3)
                            def em_a(E, ft=ft, AL=AL, qw=qw, par=par, bp=bp):
                                return E.matmul(pb[bp[0]][:, 0:qw], AL[:, ft, :],
                                                rstd_b[par][:, 0:qw], start=True,
                                                stop=True)
                            tr.op("T", em_a, reads=[f"AL{mat}", f"rstd{par}"],
                                  writes=[f"pb{bp[0]}"])
                            def em_b(E, ft=ft, BL=BL, qw=qw, par=par, bp=bp):
                                return E.matmul(pb[bp[1]][:, 0:qw], BL[:, ft, :],
                                                rhsB_b[par][:, 0:qw], start=True,
                                                stop=True)
                            tr.op("T", em_b, reads=[f"BL{mat}", f"rhsBm{par}",
                                                    f"rhsBc{par}"],
                                  writes=[f"pb{bp[1]}"])
                            half = ft % 2
                            sk = f"scr{half}"
                            def em_t1(E, ft=ft, qd=qd, qw=qw, half=half, bp=bp):
                                return E.tensor_tensor(
                                    scr[:, half * 512:half * 512 + qw],
                                    out_sb[:, ft, qd:qd + qw],
                                    pb[bp[0]][:, 0:qw], ALU.mult)
                            tr.op("V", em_t1, reads=[okey, f"pb{bp[0]}"],
                                  writes=[sk])
                            def em_t2(E, ft=ft, qd=qd, qw=qw, half=half, bp=bp):
                                return E.tensor_tensor(
                                    out_sb[:, ft, qd:qd + qw],
                                    scr[:, half * 512:half * 512 + qw],
                                    pb[bp[1]][:, 0:qw], ALU.add)
                            tr.op("V", em_t2, reads=[sk, f"pb{bp[1]}"],
                                  writes=[okey])

                    for idx in range(len(chunks)):
                        phase1(idx)
                        lnchain(idx)
                        if idx > 0:
                            apply_(idx - 1)
                    apply_(len(chunks) - 1)

                qk_pass(wqT, Qb, "Qb", bqc, "bqc", "q", ALq, BLq, True, eighth_row)
                qk_pass(wkT, Kb, "Kb", bkc, "bkc", "k", ALk, BLk, False, ones_row)

                tr.op("P", lambda E: E.dma_start(
                    out=Wbuf[:], in_=wvT.rearrange("(a p) f -> p a f", p=128)),
                    writes=["Wbuf"], dma=True)
                xchunk_load(0, 0, "xb0v")
                for ci in range(len(CH5)):
                    c0, cw = CH5[ci]
                    pp = ci % 2
                    xk = f"xb{pp}v"
                    if ci + 1 < len(CH5):
                        xchunk_load(ci + 1, (ci + 1) % 2, f"xb{(ci + 1) % 2}v")
                    for tl in range(cw // 128):
                        tt = c0 // 128 + tl
                        for fh in range(2):
                            bank = (0, 1, 6, 7)[(tt * 2 + fh) % 4]
                            def em_vmm(E, tl=tl, fh=fh, bank=bank, pp=pp):
                                inst = None
                                for fi in range(8):
                                    inst = E.matmul(pb[bank][:],
                                                    xbuf[pp][:, fi, tl * 128:(tl + 1) * 128],
                                                    Wbuf[:, fi, fh * 512:(fh + 1) * 512],
                                                    start=(fi == 0), stop=(fi == 7))
                                return inst
                            tr.op("T", em_vmm, reads=["Wbuf", xk], writes=[f"pb{bank}"])
                            def em_vev(E, tt=tt, fh=fh, bank=bank):
                                return E.scalar_tensor_tensor(
                                    Vb[:, tt, fh * 512:(fh + 1) * 512], pb[bank][:], 1.0,
                                    bvrep[:, fh * 512:(fh + 1) * 512], ALU.mult, ALU.add)
                            tr.op("V", em_vev, reads=[f"pb{bank}", "bvrep"], writes=["Vb"])
                tr.barrier()

            with ExitStack() as st1b:
                sb1b = mksb(st1b)
                Wobuf = sb1b("Wobuf", [128, 8, H], BF16)
                afm_t = sb1b("afm_t", [128, 8, 128], BF16)
                hwork = sb1b("hwork", [128, H])
                h32sb = sb1b("h32sb", [128, H])
                hbfsb = sb1b("hbfsb", [128, H], BF16)
                hfm = sb1b("hfm", [128, 8, 128])
                Pb2 = sb1b("Pb2", [128, 2, 384], BF16)
                PTs2 = sb1b("PTs2", [128, 2, 384], BF16)
                Dg2 = sb1b("Dg2", [128, 2, 128], BF16)
                nmax2 = sb1b("nmax2", [128, 2])
                den2 = sb1b("den2", [128, 2])
                rden2 = sb1b("rden2", [128, 2])
                tot16 = sb1b("tot16", [1, NTILE, 64])
                rnmax = sb1b("rnmax", [128, 1])
                tr.op("P", lambda E: E.dma_start(
                    out=Wobuf[:], in_=woT.rearrange("(a p) f -> p a f", p=128)),
                    writes=["Wo"], dma=True)
                def em_xr0(E):
                    return E.dma_start(out=xres[:, 0, :], in_=x_tm[WIN: WIN + 128, :])
                tr.op("Y", em_xr0, writes=["xres0"], dma=True)
                tr.op("V", lambda E: E.memset(lprefx[0:1, :], 0.0), writes=["lprefx"])

                SBK3 = (0, 3, 6)
                TBK = (1, 7)

                class _Defer:
                    def __init__(self):
                        self.q = []

                    def op(self, eng, emit, reads=(), writes=(), dma=False):
                        self.q.append((eng, emit, list(reads), list(writes), dma))

                pend = _Defer()

                def flush(n):
                    for _ in range(min(n, len(pend.q))):
                        eng, emit, rds, wrs, dmaf = pend.q.pop(0)
                        tr.op(eng, emit, reads=rds, writes=wrs, dma=dmaf)

                def stage_a(i0, j, h):
                    sbank = SBK3[i0 % 3]
                    mj = min(j, 2)
                    def em_s(E, j=j, h=h, sbank=sbank):
                        ft, s = h // 2, h % 2
                        return E.matmul(pb[sbank][:, 0:384],
                                        Qb[64 * s:64 * s + 64, ft,
                                           j * 128:(j + 1) * 128],
                                        Kb[64 * s:64 * s + 64, ft,
                                           j * 128:j * 128 + 384],
                                        start=True, stop=False)
                    tr.op("T", em_s, reads=["Qb", "Kb"], writes=[f"pb{sbank}"])
                    def em_msk(E, mj=mj, sbank=sbank):
                        return E.matmul(pb[sbank][:, 0:384], identb[:], masks[:, mj, :],
                                        start=False, stop=True)
                    tr.op("T", em_msk, reads=["identb", f"masks{mj}"],
                          writes=[f"pb{sbank}"])

                def stage_b1(i0, j, h):
                    sbank = SBK3[i0 % 3]
                    par = i0 % 2
                    def em_rm(E, sbank=sbank, par=par):
                        return E.reduce_max(nmax2[:, par:par + 1], pb[sbank][:, 0:384],
                                            axis=AX.X, negate=True)
                    tr.op("V", em_rm, reads=[f"pb{sbank}"], writes=[f"nmax{par}"])

                def stage_b2(i0, j, h):
                    sbank = SBK3[i0 % 3]
                    par = i0 % 2
                    def em_exp(E, sbank=sbank, par=par):
                        return E.activation(Pb2[:, par, :], pb[sbank][:, 0:384], AF.Exp,
                                            bias=nmax2[:, par:par + 1],
                                            accum_out=den2[:, par:par + 1])
                    tr.op("S", em_exp, reads=[f"pb{sbank}", f"nmax{par}"],
                          writes=[f"Pb{par}", f"den{par}"])
                    def em_rc(E, par=par):
                        return E.reciprocal(rden2[:, par:par + 1], den2[:, par:par + 1])
                    tr.op("V", em_rc, reads=[f"den{par}"], writes=[f"rden{par}"])
                    def em_dg(E, par=par):
                        return E.tensor_scalar(Dg2[:, par, :], identb[:],
                                               rden2[:, par:par + 1], None, ALU.mult)
                    tr.op("V", em_dg, reads=["identb", f"rden{par}"],
                          writes=[f"Dg{par}"])

                def stage_c1(i0, j, h):
                    par = i0 % 2
                    tbank = TBK[par]
                    def em_tp(E, par=par, tbank=tbank):
                        inst = None
                        for kk in range(3):
                            inst = E.matmul(pb[tbank][:, kk * 128:(kk + 1) * 128],
                                            Pb2[:, par, kk * 128:(kk + 1) * 128],
                                            Dg2[:, par, :], start=True, stop=True)
                        return inst
                    tr.op("T", em_tp, reads=[f"Pb{par}", f"Dg{par}"],
                          writes=[f"pb{tbank}"])
                    eng_e = "V" if par == 0 else "S"
                    def em_pte(E, par=par, tbank=tbank, eng_e=eng_e):
                        if eng_e == "V":
                            return E.tensor_copy(PTs2[:, par, :], pb[tbank][:, 0:384])
                        return E.copy(PTs2[:, par, :], pb[tbank][:, 0:384])
                    tr.op(eng_e, em_pte, reads=[f"pb{tbank}"], writes=[f"PTs{par}"])

                def stage_c2(i0, j, h):
                    par = i0 % 2
                    ft, s = h // 2, h % 2
                    reg = ft % 4
                    def em_pv(E, j=j, h=h, s=s, par=par, reg=reg):
                        inst = None
                        for kk in range(3):
                            inst = E.matmul(
                                pb[2][64 * s:64 * s + 64, reg * 128:(reg + 1) * 128],
                                Vb[:, j + kk, h * 64:(h + 1) * 64],
                                PTs2[:, par, kk * 128:(kk + 1) * 128],
                                start=(kk == 0), stop=(kk == 2),
                                tile_position=(0, 64 * s))
                        return inst
                    tr.op("T", em_pv, reads=["Vb", f"PTs{par}"], writes=[f"pb2r{reg}"])
                    if s == 1:
                        eng2 = "V" if ft % 2 == 0 else "S"
                        def em_pve(E, ft=ft, reg=reg, eng2=eng2):
                            if eng2 == "V":
                                return E.tensor_copy(
                                    afm_t[:, ft, :],
                                    pb[2][:, reg * 128:(reg + 1) * 128])
                            return E.copy(afm_t[:, ft, :],
                                          pb[2][:, reg * 128:(reg + 1) * 128])
                        tr.op(eng2, em_pve, reads=[f"pb2r{reg}"], writes=["afm"])

                def tail(j, T):
                    sl = j % 3
                    for fh in range(2):
                        bank = 4 + fh
                        def em_wo(E, fh=fh, bank=bank):
                            for kt in range(8):
                                E.matmul(pb[bank][:], afm_t[:, kt, :],
                                         Wobuf[:, kt, fh * 512:(fh + 1) * 512],
                                         start=(kt == 0), stop=False)
                            return E.matmul(pb[bank][:], onesrow_f[0:1, 0:128],
                                            borep[0:1, fh * 512:(fh + 1) * 512],
                                            start=False, stop=True)
                        T.op("T", em_wo, reads=["afm", "Wo", "borep", "onesrow_f"],
                             writes=[f"pb{bank}"] + (["pb5b", "pb5t"] if bank == 5
                                                     else []))
                    for fh in range(2):
                        bank = 4 + fh
                        def em_h1(E, fh=fh, bank=bank, sl=sl):
                            return E.scalar_tensor_tensor(
                                hwork[:, fh * 512:(fh + 1) * 512], pb[bank][:], 1.0,
                                xres[:, sl, fh * 512:(fh + 1) * 512],
                                ALU.mult, ALU.add)
                        T.op("V", em_h1, reads=[f"pb{bank}", f"xres{sl}"],
                             writes=["hwork"])
                    T.op("S", lambda E: E.activation(scr[:], hwork[:], AF.Identity,
                                                     accum_out=lnred[:, 0:1]),
                         reads=["hwork"], writes=["scr0", "scr1", "lnm"])
                    T.op("S", lambda E: E.activation(scr[:], hwork[:], AF.Square,
                                                     accum_out=lnred[:, 1:2]),
                         reads=["hwork"], writes=["scr0", "scr1", "lnv"])
                    T.op("V", lambda E: E.tensor_scalar(lnred[:, 0:1], lnred[:, 0:1],
                                                        1.0 / H, None, ALU.mult),
                         reads=["lnm"], writes=["lnm"])
                    T.op("V", lambda E: E.tensor_tensor(lnred[:, 3:4], lnred[:, 0:1],
                                                        lnred[:, 0:1], ALU.mult),
                         reads=["lnm"], writes=["lnt"])
                    T.op("V", lambda E: E.tensor_scalar(lnred[:, 1:2], lnred[:, 1:2],
                                                        1.0 / H, EPS, ALU.mult,
                                                        ALU.add),
                         reads=["lnv"], writes=["lnv"])
                    T.op("V", lambda E: E.tensor_tensor(lnred[:, 1:2], lnred[:, 1:2],
                                                        lnred[:, 3:4], ALU.subtract),
                         reads=["lnv", "lnt"], writes=["lnv"])
                    T.op("S", lambda E: E.activation(lnred[:, 2:3], lnred[:, 1:2],
                                                     AF.Ln),
                         reads=["lnv"], writes=["rstdh"])
                    T.op("S", lambda E: E.activation(lnred[:, 2:3], lnred[:, 2:3],
                                                     AF.Exp, scale=-0.5),
                         reads=["rstdh"], writes=["rstdh"])
                    T.op("V", lambda E: E.tensor_tensor(lnred[:, 3:4], lnred[:, 0:1],
                                                        lnred[:, 2:3], ALU.mult),
                         reads=["lnm", "rstdh"], writes=["lnt"])
                    T.op("V", lambda E: E.tensor_scalar(h32sb[:], hwork[:],
                                                        lnred[:, 2:3], lnred[:, 3:4],
                                                        ALU.mult, ALU.subtract),
                         reads=["hwork", "rstdh", "lnt"], writes=["h32sb"])
                    T.op("S", lambda E: E.copy(hbfsb[:], h32sb[:]),
                         reads=["h32sb"], writes=["hbfsb"])
                    def em_dh32(E, j=j):
                        return E.dma_start(out=h32_d[j * 128:(j + 1) * 128, :],
                                           in_=h32sb[:])
                    T.op("Y", em_dh32, reads=["h32sb"], writes=[f"h32d{j}"], dma=True)
                    def em_dhbf(E, j=j):
                        return E.dma_start(out=hbf_in[j * 128:(j + 1) * 128, :],
                                           in_=hbfsb[:])
                    T.op("S", em_dhbf, reads=["hbfsb"], writes=[f"hbfd{j}"], dma=True)
                    if j % 4 == 3 and j != NTILE - 1:
                        ch = j // 4
                        def em_agh(E, ch=ch):
                            return E.collective_compute(
                                "AllGather", ALU.bypass, replica_groups=RG,
                                ins=[hbf_in[ch * 512:(ch + 1) * 512, :]],
                                outs=[hall[ch * 4096:(ch + 1) * 4096, :]])
                        T.op("P", em_agh,
                             reads=[f"hbfd{jj}" for jj in range(ch * 4, ch * 4 + 4)],
                             writes=[f"hall_d{ch}"])
                    for kt in range(8):
                        tb = 4 if kt % 2 == 0 else 5
                        def em_ht(E, kt=kt, tb=tb):
                            return E.transpose(pb[tb][:, 0:128],
                                               h32sb[:, kt * 128:(kt + 1) * 128],
                                               identf[:])
                        T.op("T", em_ht, reads=["h32sb", "identf"],
                             writes=[f"pb{tb}"] + (["pb5b", "pb5t"] if tb == 5
                                                   else []))
                        enghe = "V" if kt % 2 == 0 else "S"
                        def em_he(E, kt=kt, tb=tb, enghe=enghe):
                            if enghe == "V":
                                return E.tensor_copy(hfm[:, kt, :], pb[tb][:, 0:128])
                            return E.copy(hfm[:, kt, :], pb[tb][:, 0:128])
                        T.op(enghe, em_he, reads=[f"pb{tb}"], writes=["hfm"])
                    def em_lg(E):
                        inst = None
                        for kt in range(8):
                            inst = E.matmul(pb[4][:, 0:64], hfm[:, kt, :],
                                            gtsb[:, kt, :],
                                            start=(kt == 0), stop=(kt == 7))
                        return inst
                    T.op("T", em_lg, reads=["hfm", "gtsb"], writes=["pb4"])
                    def em_le(E, j=j):
                        return E.tensor_copy(logits[:, j, :], pb[4][:, 0:64])
                    T.op("V", em_le, reads=["pb4"], writes=[f"lg{j}"])
                    def em_nm(E, j=j):
                        return E.reduce_max(rnmax[:, 0:1], logits[:, j, :], axis=AX.X,
                                            negate=True)
                    T.op("V", em_nm, reads=[f"lg{j}"], writes=["rnmax"])
                    def em_ex(E, j=j):
                        return E.activation(Esb[:], logits[:, j, :], AF.Exp,
                                            bias=rnmax[:, 0:1])
                    T.op("S", em_ex, reads=[f"lg{j}", "rnmax"], writes=["Esb"])
                    T.op("V", lambda E: E.max(w8v[:], Esb[:]), reads=["Esb"],
                         writes=["w8v"])
                    T.op("V", lambda E: E.max_index(idx8u[:], w8v[:], Esb[:]),
                         reads=["w8v", "Esb"], writes=["idx8u"])
                    def em_i8(E, j=j):
                        return E.tensor_copy(idx8a[:, j, :], idx8u[:])
                    T.op("V", em_i8, reads=["idx8u"], writes=["idx8a"])
                    T.op("V", lambda E: E.tensor_copy(Ec[:], Esb[:]), reads=["Esb"],
                         writes=["Ec"])
                    T.op("V", lambda E: E.match_replace(Ec[:], w8v[:], Ec[:], -1.0e30),
                         reads=["w8v", "Ec"], writes=["Ec"])
                    T.op("V", lambda E: E.tensor_scalar(Rb[:], Ec[:], -1.0e29, None,
                                                        ALU.is_le),
                         reads=["Ec"], writes=["Rb"])
                    T.op("V", lambda E: E.reduce_sum(ws[:, 0:1], w8v[:], axis=AX.X),
                         reads=["w8v"], writes=["ws"])
                    T.op("V", lambda E: E.reciprocal(rws[:, 0:1], ws[:, 0:1]),
                         reads=["ws"], writes=["rws"])
                    def em_w8(E, j=j):
                        return E.tensor_scalar(w8na[:, j, :], w8v[:], rws[:, 0:1],
                                               None, ALU.mult)
                    T.op("V", em_w8, reads=["w8v", "rws"], writes=["w8na"])
                    def em_pe(E):
                        return E.matmul(pb[5][:, 0:64], trib[:], Rb[:], start=True,
                                        stop=False)
                    T.op("T", em_pe, reads=["trib", "Rb"], writes=["pb5"])
                    def em_lb(E):
                        return E.matmul(pb[5][:, 0:64], onesrow_f[0:1, 0:128],
                                        lprefx[0:1, 0:64], start=False, stop=True)
                    T.op("T", em_lb, reads=["lprefx", "onesrow_f"], writes=["pb5"])
                    def em_tt(E):
                        return E.matmul(pb[5][0:1, 64:128], onescol_b[:], Rb[:],
                                        start=True, stop=True)
                    T.op("T", em_tt, reads=["onescol_b", "Rb"], writes=["pb5t"])
                    def em_te(E, j=j):
                        return E.tensor_copy(tot16[0:1, j, :], pb[5][0:1, 64:128])
                    T.op("V", em_te, reads=["pb5t"], writes=["tot16"])
                    T.op("V", lambda E: E.tensor_copy(posg[:], pb[5][:, 0:64]),
                         reads=["pb5"], writes=["posg"])
                    def em_if(E, j=j):
                        return E.tensor_copy(idx8f[:], idx8a[:, j, :])
                    T.op("V", em_if, reads=["idx8a"], writes=["idx8f"])
                    for k in range(8):
                        def em_pk(E, j=j, k=k):
                            return E.scalar_tensor_tensor(
                                junk64[:], iota64r[:], idx8f[:, k:k + 1], posg[:],
                                ALU.is_equal, ALU.mult,
                                accum_out=lposk8[:, j, k:k + 1])
                        T.op("V", em_pk, reads=["iota64r", "idx8f", "posg"],
                             writes=["junk64", "lposk8"])
                    def em_v8(E, j=j):
                        return E.tensor_scalar(valid8[:], lposk8[:, j, :], CAP - 0.5,
                                               None, ALU.is_lt)
                    T.op("V", em_v8, reads=["lposk8"], writes=["valid8"])
                    T.op("V", lambda E: E.tensor_scalar(idx256[:], idx8f[:], 256.0,
                                                        None, ALU.mult),
                         reads=["idx8f"], writes=["idx256"])
                    pp = j % 4
                    def em_p0(E, j=j, pp=pp):
                        return E.tensor_copy(
                            pays[pp][:, :, 0:1],
                            tids1[:, j:j + 1].unsqueeze(1).broadcast_to([128, 8, 1]))
                    T.op("V", em_p0, reads=["tids1"], writes=[f"pay{pp}"])
                    def em_p1(E, j=j, pp=pp):
                        return E.tensor_copy(pays[pp][:, :, 1], w8na[:, j, :])
                    T.op("V", em_p1, reads=["w8na"], writes=[f"pay{pp}"])
                    def em_s0(E, j=j):
                        return E.tensor_tensor(soff8[:], idx256[:], lposk8[:, j, :],
                                               ALU.add)
                    T.op("V", em_s0, reads=["idx256", "lposk8"], writes=["soff8"])
                    T.op("V", lambda E: E.tensor_tensor(soff8[:], soff8[:], valid8[:],
                                                        ALU.mult),
                         reads=["soff8", "valid8"], writes=["soff8"])
                    T.op("V", lambda E: E.tensor_scalar(tmp8[:], valid8[:], -1.0, 1.0,
                                                        ALU.mult, ALU.add),
                         reads=["valid8"], writes=["tmp8"])
                    T.op("V", lambda E: E.tensor_scalar(tmp8[:], tmp8[:],
                                                        trashcol[:, 0:1], None,
                                                        ALU.mult),
                         reads=["tmp8", "trashcol"], writes=["tmp8"])
                    T.op("V", lambda E: E.tensor_tensor(soff8[:], soff8[:], tmp8[:],
                                                        ALU.add),
                         reads=["soff8", "tmp8"], writes=["soff8"])
                    def em_su(E, pp=pp):
                        return E.tensor_copy(soffu[pp][:], soff8[:])
                    T.op("V", em_su, reads=["soff8"], writes=[f"soffu{pp}"])
                    for k in range(8):
                        def em_sc(E, k=k, pp=pp, j=j):
                            return E.indirect_dma_start(
                                out=ts_in,
                                out_offset=IndirectOffsetOnAxis(
                                    ap=soffu[pp][:, k:k + 1], axis=0),
                                in_=pays[pp][:, k, :], in_offset=None)
                        T.op("P", em_sc, reads=[f"soffu{pp}", f"pay{pp}", "tsz"],
                             writes=[f"sc{j}_{k}"], dma=True)
                    def em_lu(E, j=j):
                        return E.tensor_tensor(lprefx[0:1, :], lprefx[0:1, :],
                                               tot16[0:1, j, :], ALU.add)
                    T.op("V", em_lu, reads=["tot16", "lprefx"], writes=["lprefx"])

                steps = [(j, h) for j in range(NTILE) for h in range(NH)]
                NSTEP = len(steps)
                for i in range(NSTEP + 4):
                    if i < NSTEP:
                        j, h = steps[i]
                        if h == 0 and j + 1 < NTILE:
                            def em_xr(E, j=j):
                                return E.dma_start(
                                    out=xres[:, (j + 1) % 3, :],
                                    in_=x_tm[WIN + (j + 1) * 128:
                                             WIN + (j + 2) * 128, :])
                            tr.op("Y", em_xr, writes=[f"xres{(j + 1) % 3}"], dma=True)
                        stage_a(i, j, h)
                    if 1 <= i < NSTEP + 1:
                        stage_b1(i - 1, *steps[i - 1])
                    if 2 <= i < NSTEP + 2:
                        stage_b2(i - 2, *steps[i - 2])
                    if 3 <= i < NSTEP + 3:
                        stage_c1(i - 3, *steps[i - 3])
                    if 4 <= i < NSTEP + 4:
                        j0, h0 = steps[i - 4]
                        stage_c2(i - 4, j0, h0)
                        if h0 == NH - 1:
                            tail(j0, pend)
                    flush(5)
                flush(len(pend.q))
                tr.op("V", lambda E: E.tensor_copy(ctot[0:1, :], lprefx[0:1, :]),
                      reads=["lprefx"], writes=["ctot"])
                tr.op("S", lambda E: E.dma_start(out=tot_in, in_=ctot[:]),
                      reads=["ctot"], writes=["totin_d"], dma=True)
                tr.op("P", lambda E: E.collective_compute(
                    "AllGather", ALU.bypass, replica_groups=RG, ins=[tot_in],
                    outs=[tot_out]),
                    reads=["totin_d"], writes=["totout_d"])
                def em_agh3(E):
                    return E.collective_compute(
                        "AllGather", ALU.bypass, replica_groups=RG,
                        ins=[hbf_in[3 * 512:4 * 512, :]],
                        outs=[hall[3 * 4096:4 * 4096, :]])
                tr.op("P", em_agh3, reads=[f"hbfd{jj}" for jj in range(12, 16)],
                      writes=["hall_d3"])

                tr.barrier()
            tr.barrier()

        tr.op("Y", lambda E: E.dma_start(out=totall[:], in_=tot_out), reads=["totout_d"],
              writes=["totall"], dma=True)
        tr.op("T", lambda E: E.matmul(pb[2][0:1, 0:64], cmsk[:], totall[:],
                                      start=True, stop=True),
              reads=["cmsk", "totall"], writes=["pb2"])
        tr.op("V", lambda E: E.tensor_copy(coreoff[:], pb[2][0:1, 0:64]),
              reads=["pb2"], writes=["coreoff"])

        with ExitStack() as st4:
            sb4 = mksb(st4)
            Dtm = sb4("Dtm", [128, 16, H], BF16)
            Dfm = [sb4(f"Dfm{i}", [128, 8, 256], BF16) for i in range(2)]
            Hfm = sb4("Hfm", [128, 6, 256], BF16)
            gsil = sb4("gsil", [128, 2, 256])
            Ysb = sb4("Ysb", [128, 2, 512], BF16)
            wg_sb = [sb4(f"wg{i}", [128, 8, INTER], BF16) for i in range(2)]
            wu_sb = [sb4(f"wu{i}", [128, 8, INTER], BF16) for i in range(2)]
            wd_sb = [sb4(f"wd{i}", [128, 6, H], BF16) for i in range(2)]
            zrow = sb4("zrow", [1, H], BF16)
            coreoffb = sb4("coreoffb", [128, 64])
            agT = sb4("agT", [128, 16], U32)
            lm9sb = sb4("lm9sb", [8, 9])
            sel64sb = sb4("sel64sb", [64, 8])
            CTsb = sb4("CTsb", [64, 9])
            gposf = sb4("gposf", [64, 256])
            e256f = sb4("e256f", [64, 256])
            rcA = sb4("rcA", [64, 256]); rcB = sb4("rcB", [64, 256])
            rsA = sb4("rsA", [64, 256]); rsB = sb4("rsB", [64, 256])
            rtq = sb4("rtq", [64, 256]); remp = sb4("remp", [64, 256])
            ragw = sb4("ragw", [64, 256])

            def load_ew(e, slot):
                def em_g(E, e=e, slot=slot):
                    return E.dma_start(out=wg_sb[slot][:],
                                       in_=wgT[e].rearrange("(a p) f -> p a f", p=128))
                tr.op("P", em_g, writes=[f"wg{slot}"], dma=True)
                def em_u(E, e=e, slot=slot):
                    return E.dma_start(out=wu_sb[slot][:],
                                       in_=wuT[e].rearrange("(a p) f -> p a f", p=128))
                tr.op("P", em_u, writes=[f"wu{slot}"], dma=True)
                def em_d(E, e=e, slot=slot):
                    return E.dma_start(out=wd_sb[slot][:],
                                       in_=wdT[e].rearrange("(a p) f -> p a f", p=128))
                tr.op("P", em_d, writes=[f"wd{slot}"], dma=True)

            tr.op("P", lambda E: E.collective_compute(
                "AllGather", ALU.bypass, replica_groups=RG, ins=[ts_in], outs=[ts_all]),
                reads=[f"sc{j}_{k}" for j in range(NTILE) for k in range(8)] + ["tsz"],
                writes=["tsall_d"])
            load_ew(0, 0)
            load_ew(1, 1)

            bcast_row(coreoffb, coreoff, 128, 64, "coreoff", "coreoffb")
            tr.op("Y", lambda E: E.dma_start(out=lm9sb[:], in_=lmask9), writes=["lm9sb"],
                  dma=True)
            tr.op("Y", lambda E: E.dma_start(out=sel64sb[:], in_=sel64), writes=["sel64sb"],
                  dma=True)
            tr.op("T", lambda E: E.matmul(pb[3][0:64, 0:9], totall[:], lm9sb[:],
                                          start=True, stop=True),
                  reads=["totall", "lm9sb"], writes=["pb3"])
            tr.op("V", lambda E: E.tensor_copy(CTsb[:], pb[3][0:64, 0:9]),
                  reads=["pb3"], writes=["CTsb"])
            tr.op("P", lambda E: E.iota(gposf[:], [[1, 256]], base=0, channel_multiplier=0,
                                        allow_small_or_imprecise_dtypes=True),
                  writes=["gposf"])
            tr.op("P", lambda E: E.iota(e256f[:], [[0, 256]], base=0,
                                        channel_multiplier=256,
                                        allow_small_or_imprecise_dtypes=True),
                  writes=["e256f"])
            rcb = {"rcA": rcA, "rcB": rcB}
            rsb_m = {"rsA": rsA, "rsB": rsB}
            tr.op("V", lambda E: E.tensor_scalar(rcA[:], gposf[:], CTsb[:, 1:2], None,
                                                 ALU.is_ge),
                  reads=["gposf", "CTsb"], writes=["rcA"])
            rc_cur = "rcA"
            for cp in range(2, 9):
                nxt = "rcB" if rc_cur == "rcA" else "rcA"
                def em_rc(E, cp=cp, rc_cur=rc_cur, nxt=nxt):
                    return E.scalar_tensor_tensor(rcb[nxt][:], gposf[:],
                                                  CTsb[:, cp:cp + 1], rcb[rc_cur][:],
                                                  ALU.is_ge, ALU.add)
                tr.op("V", em_rc, reads=["gposf", "CTsb", rc_cur], writes=[nxt])
                rc_cur = nxt
            tr.op("V", lambda E: E.memset(rsA[:], 0.0), writes=["rsA"])
            rs_cur = "rsA"
            for cp in range(1, 9):
                def em_oq(E, cp=cp, rc_cur=rc_cur):
                    return E.tensor_scalar(rtq[:], rcb[rc_cur][:], float(cp), None,
                                           ALU.is_equal)
                tr.op("V", em_oq, reads=[rc_cur], writes=["rtq"])
                nxt = "rsB" if rs_cur == "rsA" else "rsA"
                def em_sq(E, cp=cp, rs_cur=rs_cur, nxt=nxt):
                    return E.scalar_tensor_tensor(rsb_m[nxt][:], rtq[:],
                                                  CTsb[:, cp:cp + 1], rsb_m[rs_cur][:],
                                                  ALU.mult, ALU.add)
                tr.op("V", em_sq, reads=["rtq", "CTsb", rs_cur], writes=[nxt])
                rs_cur = nxt
            def em_g1(E, rs_cur=rs_cur):
                return E.tensor_tensor(ragw[:], gposf[:], rsb_m[rs_cur][:], ALU.subtract)
            tr.op("V", em_g1, reads=["gposf", rs_cur], writes=["ragw"])
            tr.op("V", lambda E: E.tensor_tensor(ragw[:], ragw[:], e256f[:], ALU.add),
                  reads=["ragw", "e256f"], writes=["ragw"])
            def em_g2(E, rc_cur=rc_cur):
                return E.scalar_tensor_tensor(ragw[:], rcb[rc_cur][:], float(TSZ), ragw[:],
                                              ALU.mult, ALU.add)
            tr.op("V", em_g2, reads=[rc_cur, "ragw"], writes=["ragw"])
            tr.op("V", lambda E: E.tensor_scalar(remp[:], gposf[:], CTsb[:, 8:9], None,
                                                 ALU.is_ge),
                  reads=["gposf", "CTsb"], writes=["remp"])
            tr.op("V", lambda E: E.tensor_scalar(rtq[:], remp[:], -1.0, 1.0, ALU.mult,
                                                 ALU.add),
                  reads=["remp"], writes=["rtq"])
            tr.op("V", lambda E: E.tensor_tensor(ragw[:], ragw[:], rtq[:], ALU.mult),
                  reads=["ragw", "rtq"], writes=["ragw"])
            tr.op("V", lambda E: E.tensor_scalar(rtq[:], remp[:], float(TALL + 128), None,
                                                 ALU.mult),
                  reads=["remp"], writes=["rtq"])
            tr.op("V", lambda E: E.tensor_tensor(ragw[:], ragw[:], rtq[:], ALU.add),
                  reads=["ragw", "rtq"], writes=["ragw"])
            for hi in range(2):
                bk = 2 if hi == 0 else 3
                def em_sel(E, hi=hi, bk=bk):
                    return E.matmul(pb[bk][:, 0:8], ragw[:, hi * 128:(hi + 1) * 128],
                                    sel64sb[:], start=True, stop=True)
                tr.op("T", em_sel, reads=["ragw", "sel64sb"], writes=[f"pb{bk}"])
                def em_agt(E, hi=hi, bk=bk):
                    return E.tensor_copy(agT[:, hi * 8:(hi + 1) * 8], pb[bk][:, 0:8])
                tr.op("V", em_agt, reads=[f"pb{bk}"], writes=["agT"])
            for g in range(16):
                col = (g % 2) * 8 + g // 2
                def em_twg(E, g=g, col=col):
                    return E.indirect_dma_start(
                        out=tw_my[:, g, :], out_offset=None, in_=ts_all,
                        in_offset=IndirectOffsetOnAxis(ap=agT[:, col:col + 1], axis=0))
                tr.op("P", em_twg, reads=["agT", "tsall_d"], writes=["tw_my"], dma=True)
            tr.op("V", lambda E: E.tensor_scalar(hmsk[:], tw_my[:, :, 0], 0.5,
                                                 float(TALL + 1), ALU.is_le, ALU.mult),
                  reads=["tw_my"], writes=["hmsk"])
            tr.op("V", lambda E: E.tensor_scalar(hidxf[:], tw_my[:, :, 0], -1.0, None,
                                                 ALU.add),
                  reads=["tw_my"], writes=["hidxf"])
            tr.op("V", lambda E: E.tensor_tensor(hidxf[:], hidxf[:], hmsk[:], ALU.add),
                  reads=["hidxf", "hmsk"], writes=["hidxf"])
            tr.op("V", lambda E: E.tensor_copy(hidxu[:], hidxf[:]),
                  reads=["hidxf"], writes=["hidxu"])

            tr.op("V", lambda E: E.memset(zrow[:], 0.0), writes=["zrow"])
            tr.op("P", lambda E: E.dma_start(out=hall[TALL:TALL + 1, :], in_=zrow[:]),
                  reads=["zrow"], writes=["hallz"], dma=True)
            for g in range(16):
                def em_dg2(E, g=g):
                    return E.indirect_dma_start(
                        out=Dtm[:, g, :], out_offset=None, in_=hall,
                        in_offset=IndirectOffsetOnAxis(ap=hidxu[:, g:g + 1], axis=0))
                tr.op("P", em_dg2,
                      reads=["hidxu", "hallz"] + [f"hall_d{c}" for c in range(4)],
                      writes=[f"Dtm{g}"], dma=True)

            for e in range(EPL):
                slot = e % 2
                for ct in range(2):
                    g = 2 * e + ct
                    for ftt in range(8):
                        tb = 0 if ftt % 2 == 0 else 1
                        def em_dt(E, g=g, ftt=ftt, tb=tb):
                            return E.transpose(pbf(tb, 128),
                                               Dtm[:, g, ftt * 128:(ftt + 1) * 128],
                                               identb[:])
                        tr.op("T", em_dt, reads=[f"Dtm{g}", "identb"], writes=[f"pb{tb}"])
                        eng = "V" if ftt % 2 == 0 else "S"
                        def em_de(E, ct=ct, ftt=ftt, tb=tb, slot=slot, eng=eng):
                            if eng == "V":
                                return E.tensor_copy(
                                    Dfm[slot][:, ftt, ct * 128:(ct + 1) * 128], pbf(tb, 128))
                            return E.copy(Dfm[slot][:, ftt, ct * 128:(ct + 1) * 128],
                                          pbf(tb, 128))
                        tr.op(eng, em_de, reads=[f"pb{tb}"], writes=[f"Dfm{slot}"])
                for it in range(6):
                    gb = (2, 6)[it % 2]
                    ub = (3, 7)[it % 2]
                    def em_g(E, it=it, slot=slot, gb=gb):
                        inst = None
                        for ftt in range(8):
                            inst = E.matmul(pb[gb][:, 0:256],
                                            wg_sb[slot][:, ftt, it * 128:(it + 1) * 128],
                                            Dfm[slot][:, ftt, :],
                                            start=(ftt == 0), stop=(ftt == 7))
                        return inst
                    tr.op("T", em_g, reads=[f"wg{slot}", f"Dfm{slot}"],
                          writes=[f"pb{gb}"])
                    def em_u(E, it=it, slot=slot, ub=ub):
                        inst = None
                        for ftt in range(8):
                            inst = E.matmul(pb[ub][:, 0:256],
                                            wu_sb[slot][:, ftt, it * 128:(it + 1) * 128],
                                            Dfm[slot][:, ftt, :],
                                            start=(ftt == 0), stop=(ftt == 7))
                        return inst
                    tr.op("T", em_u, reads=[f"wu{slot}", f"Dfm{slot}"],
                          writes=[f"pb{ub}"])
                    def em_sl(E, it=it, gb=gb):
                        return E.activation(gsil[:, it % 2, :], pb[gb][:, 0:256],
                                            AF.Silu)
                    tr.op("S", em_sl, reads=[f"pb{gb}"], writes=[f"gsil{it % 2}"])
                    def em_hp(E, it=it, ub=ub):
                        return E.tensor_tensor(Hfm[:, it, :], gsil[:, it % 2, :],
                                               pb[ub][:, 0:256], ALU.mult)
                    tr.op("V", em_hp, reads=[f"gsil{it % 2}", f"pb{ub}"],
                          writes=["Hfm"])
                for ct in range(2):
                    g = 2 * e + ct
                    for fh in range(2):
                        bank = 4 + fh
                        def em_dn(E, ct=ct, fh=fh, slot=slot, bank=bank):
                            inst = None
                            for it in range(6):
                                inst = E.matmul(pb[bank][:],
                                                Hfm[:, it, ct * 128:(ct + 1) * 128],
                                                wd_sb[slot][:, it, fh * 512:(fh + 1) * 512],
                                                start=(it == 0), stop=(it == 5))
                            return inst
                        tr.op("T", em_dn, reads=[f"wd{slot}", "Hfm"], writes=[f"pb{bank}"])
                        def em_ev(E, g=g, fh=fh, bank=bank):
                            return E.tensor_scalar(Ysb[:, fh, :], pb[bank][:],
                                                   tw_my[:, g, 1:2], None, ALU.mult)
                        tr.op("V", em_ev, reads=[f"pb{bank}", "tw_my"], writes=[f"Ysb{fh}"])
                        def em_st(E, g=g, fh=fh):
                            return E.dma_start(out=yexp_in[g * 128:(g + 1) * 128,
                                                           fh * 512:(fh + 1) * 512],
                                               in_=Ysb[:, fh, :])
                        tr.op("S", em_st, reads=[f"Ysb{fh}"], writes=[f"yst{g}_{fh}"],
                              dma=True)
                if e + 2 < EPL:
                    load_ew(e + 2, e % 2)
                for jt in (2 * e, 2 * e + 1):
                    def em_if2(E, j=jt):
                        return E.tensor_copy(idx8f[:], idx8a[:, j, :])
                    tr.op("V", em_if2, reads=["idx8a"], writes=["idx8f"])
                    for k in range(8):
                        def em_oh2(E, k=k):
                            return E.tensor_scalar(ohk[:], iota64r[:], idx8f[:, k:k + 1],
                                                   None, ALU.is_equal)
                        tr.op("V", em_oh2, reads=["iota64r", "idx8f"], writes=["ohk"])
                        def em_ck(E, k=k):
                            return E.scalar_tensor_tensor(junk64[:], coreoffb[:], 1.0,
                                                          ohk[:], ALU.mult, ALU.mult,
                                                          accum_out=posk8[:, k:k + 1])
                        tr.op("V", em_ck, reads=["coreoffb", "ohk"],
                              writes=["junk64", "posk8"])
                        def em_ek2(E, k=k):
                            return E.scalar_tensor_tensor(junk64[:], erowb[:], 1.0,
                                                          ohk[:], ALU.mult, ALU.mult,
                                                          accum_out=erowk8[:, k:k + 1])
                        tr.op("V", em_ek2, reads=["erowb", "ohk"],
                              writes=["junk64", "erowk8"])
                    def em_gp(E, j=jt):
                        return E.tensor_tensor(posk8[:], posk8[:], lposk8[:, j, :],
                                               ALU.add)
                    tr.op("V", em_gp, reads=["posk8", "lposk8"], writes=["posk8"])
                    tr.op("V", lambda E: E.tensor_scalar(valid8[:], posk8[:], CAP - 0.5,
                                                         None, ALU.is_lt),
                          reads=["posk8"], writes=["valid8"])
                    tr.op("V", lambda E: E.tensor_scalar(coff8[:], posk8[:], CAP - 1.0,
                                                         None, ALU.min),
                          reads=["posk8"], writes=["coff8"])
                    def em_ce(E, j=jt):
                        return E.tensor_tensor(coff8[:], coff8[:], erowk8[:], ALU.add)
                    tr.op("V", em_ce, reads=["coff8", "erowk8"], writes=["coff8"])
                    tr.op("V", lambda E: E.tensor_tensor(coff8[:], coff8[:], valid8[:],
                                                         ALU.mult),
                          reads=["coff8", "valid8"], writes=["coff8"])
                    tr.op("V", lambda E: E.tensor_scalar(tmp8[:], valid8[:], -float(TALL),
                                                         float(TALL), ALU.mult, ALU.add),
                          reads=["valid8"], writes=["tmp8"])
                    tr.op("V", lambda E: E.tensor_tensor(coff8[:], coff8[:], tmp8[:],
                                                         ALU.add),
                          reads=["coff8", "tmp8"], writes=["coff8"])
                    def em_cu2(E, j=jt):
                        return E.tensor_copy(offs_all[:, j, :], coff8[:])
                    tr.op("V", em_cu2, reads=["coff8"], writes=[f"offs{jt}"])
                if e % 4 == 3:
                    yc = e // 4
                    def em_agy(E, yc=yc):
                        return E.collective_compute(
                            "AllGather", ALU.bypass, replica_groups=RG,
                            ins=[yexp_in[yc * 1024:(yc + 1) * 1024, :]],
                            outs=[yall[yc * 8192:(yc + 1) * 8192, :]])
                    tr.op("P", em_agy,
                          reads=[f"yst{g}_{fh}" for g in range(yc * 8, yc * 8 + 8)
                                 for fh in range(2)],
                          writes=[f"yall_d{yc}"])

            tr.op("P", lambda E: E.dma_start(out=yall[TALL:TALL + 1, :], in_=zrow[:]),
                  reads=["zrow"], writes=["yzpad"], dma=True)
            tr.barrier()

        with ExitStack() as st5:
            sb5 = mksb(st5)
            gdp = [sb5(f"gdp{i}", [128, 8, H]) for i in range(2)]
            s_a = sb5("s_a", [128, H]); s_c = sb5("s_c", [128, H])
            s_b = sb5("s_b", [128, H]); s_d = sb5("s_d", [128, H])
            outw = sb5("outw", [128, H])
            def em_hr0(E):
                return E.dma_start(out=xres[:, 0, :], in_=h32_d[0:128, :])
            tr.op("Y", em_hr0, reads=["h32d0"], writes=["xres0"], dma=True)
            for j in range(NTILE):
                sl = j % 2
                if j + 1 < NTILE:
                    def em_hr(E, j=j):
                        return E.dma_start(out=xres[:, (j + 1) % 2, :],
                                           in_=h32_d[(j + 1) * 128:(j + 2) * 128, :])
                    tr.op("Y", em_hr, reads=[f"h32d{j + 1}"],
                          writes=[f"xres{(j + 1) % 2}"], dma=True)
                for kk in range(8):
                    def em_g0(E, j=j, kk=kk, sl=sl):
                        return E.indirect_dma_start(
                            out=gdp[sl][:, kk, :], out_offset=None, in_=yall,
                            in_offset=IndirectOffsetOnAxis(
                                ap=offs_all[:, j, kk:kk + 1], axis=0))
                    tr.op("P", em_g0,
                          reads=[f"offs{j}", "yall_d0", "yall_d1", "yzpad"],
                          writes=[f"gdp{sl}_{kk}"], dma=True)
                for dst, key, a, bq_ in ((s_a, "s_a", 0, 1), (s_b, "s_b", 2, 3),
                                         (s_c, "s_c", 4, 5), (s_d, "s_d", 6, 7)):
                    def em_s1(E, sl=sl, dst=dst, a=a, bq_=bq_):
                        return E.tensor_tensor(dst[:], gdp[sl][:, a, :],
                                               gdp[sl][:, bq_, :], ALU.add)
                    rds = ([f"gdp{sl}_{t}" for t in range(8)] if a == 0
                           else [f"gdp{sl}_{a}", f"gdp{sl}_{bq_}"])
                    tr.op("V", em_s1, reads=rds, writes=[key])
                tr.op("V", lambda E: E.tensor_tensor(s_a[:], s_a[:], s_b[:], ALU.add),
                      reads=["s_a", "s_b"], writes=["s_a"])
                tr.op("V", lambda E: E.tensor_tensor(s_c[:], s_c[:], s_d[:], ALU.add),
                      reads=["s_c", "s_d"], writes=["s_c"])
                tr.op("V", lambda E: E.tensor_tensor(s_a[:], s_a[:], s_c[:], ALU.add),
                      reads=["s_a", "s_c"], writes=["s_a"])
                def em_res(E, sl=sl):
                    return E.tensor_tensor(outw[:], s_a[:], xres[:, sl, :], ALU.add)
                tr.op("V", em_res, reads=["s_a", f"xres{sl}"], writes=["outw"])
                tr.op("S", lambda E: E.activation(scr[:], outw[:], AF.Identity,
                                                  accum_out=lnred[:, 0:1]),
                      reads=["outw"], writes=["scr0", "scr1", "lnm"])
                tr.op("S", lambda E: E.activation(scr[:], outw[:], AF.Square,
                                                  accum_out=lnred[:, 1:2]),
                      reads=["outw"], writes=["scr0", "scr1", "lnv"])
                tr.op("V", lambda E: E.tensor_scalar(lnred[:, 0:1], lnred[:, 0:1], 1.0 / H,
                                                     None, ALU.mult),
                      reads=["lnm"], writes=["lnm"])
                tr.op("V", lambda E: E.tensor_tensor(lnred[:, 3:4], lnred[:, 0:1],
                                                     lnred[:, 0:1], ALU.mult),
                      reads=["lnm"], writes=["lnt"])
                tr.op("V", lambda E: E.tensor_scalar(lnred[:, 1:2], lnred[:, 1:2], 1.0 / H,
                                                     EPS, ALU.mult, ALU.add),
                      reads=["lnv"], writes=["lnv"])
                tr.op("V", lambda E: E.tensor_tensor(lnred[:, 1:2], lnred[:, 1:2],
                                                     lnred[:, 3:4], ALU.subtract),
                      reads=["lnv", "lnt"], writes=["lnv"])
                tr.op("S", lambda E: E.activation(lnred[:, 2:3], lnred[:, 1:2], AF.Ln),
                      reads=["lnv"], writes=["rstdh"])
                tr.op("S", lambda E: E.activation(lnred[:, 2:3], lnred[:, 2:3], AF.Exp,
                                                  scale=-0.5),
                      reads=["rstdh"], writes=["rstdh"])
                tr.op("V", lambda E: E.tensor_tensor(lnred[:, 3:4], lnred[:, 0:1],
                                                     lnred[:, 2:3], ALU.mult),
                      reads=["lnm", "rstdh"], writes=["lnt"])
                tr.op("V", lambda E: E.tensor_scalar(outw[:], outw[:], lnred[:, 2:3],
                                                     lnred[:, 3:4], ALU.mult,
                                                     ALU.subtract),
                      reads=["outw", "rstdh", "lnt"], writes=["outw"])
                def em_yo(E, j=j):
                    return E.dma_start(out=y_out[j * 128:(j + 1) * 128, :], in_=outw[:])
                tr.op("S", em_yo, reads=["outw"], writes=[f"yo{j}"], dma=True)

            tr.wait_all_dma()
            tr.build(block)
    return nc


_NC_CACHE = None


def kernel(**inputs):
    global _NC_CACHE
    x = np.asarray(inputs["L0_attn_x"], np.float32)
    Wq = np.ascontiguousarray(np.asarray(inputs["L0_attn_Wq"], np.float32).T)
    Wk = np.ascontiguousarray(np.asarray(inputs["L0_attn_Wk"], np.float32).T)
    Wv = np.ascontiguousarray(np.asarray(inputs["L0_attn_Wv"], np.float32).T)
    Wo = np.ascontiguousarray(np.asarray(inputs["L0_attn_Wo"], np.float32).T)
    bq = np.asarray(inputs["L0_attn_bq"], np.float32)
    bk = np.asarray(inputs["L0_attn_bk"], np.float32)
    bv = np.asarray(inputs["L0_attn_bv"], np.float32)
    bo = np.asarray(inputs["L0_attn_bo"], np.float32)
    lnqw = np.asarray(inputs["L0_attn_ln_q_w"], np.float32)
    lnqb = np.asarray(inputs["L0_attn_ln_q_b"], np.float32)
    lnkw = np.asarray(inputs["L0_attn_ln_k_w"], np.float32)
    lnkb = np.asarray(inputs["L0_attn_ln_k_b"], np.float32)
    gate = np.asarray(inputs["L0_ffn_gate_w"], np.float32)
    wgate = np.asarray(inputs["L0_ffn_w_gate"], np.float32)
    wup = np.asarray(inputs["L0_ffn_w_up"], np.float32)
    wdown = np.asarray(inputs["L0_ffn_w_down"], np.float32)

    B, S, _ = x.shape
    gateT = np.ascontiguousarray(gate.T)
    wgTa = np.ascontiguousarray(wgate.transpose(0, 2, 1))
    wuTa = np.ascontiguousarray(wup.transpose(0, 2, 1))
    wdTa = np.ascontiguousarray(wdown.transpose(0, 2, 1))
    tri = np.triu(np.ones((128, 128), np.float32), 1)
    ident = np.eye(128, dtype=np.float32)
    ones_row = np.ones((1, 512), np.float32)
    eighth_row = np.full((1, 512), 0.125, np.float32)
    ee = np.arange(NEXP)
    erow_np = ((ee % 8 // 4) * 8192 + (ee // 8) * 1024 +
               (ee % 8 % 4) * 256).astype(np.float32).reshape(1, NEXP)
    bq_fm = np.ascontiguousarray(bq.reshape(8, 128).T)
    bk_fm = np.ascontiguousarray(bk.reshape(8, 128).T)

    in_maps = []
    for c in range(NCORES):
        b, half = c // 2, c % 2
        lo = half * TCORE - WIN
        if lo < 0:
            xc = np.concatenate([np.zeros((WIN, H), np.float32),
                                 x[b, 0:half * TCORE + TCORE]], axis=0)
        else:
            xc = x[b, lo:half * TCORE + TCORE]
        xc = np.ascontiguousarray(xc)
        tvals_c = np.array([[256.0, 128.0]] if half == 0 else [[0.0, 0.0]], np.float32)
        t0 = c * TCORE
        jj = np.arange(NTILE)[None, :]
        tids_c = ((jj // 4) * 4096 + c * 512 + (jj % 4) * 128 +
                  np.arange(128)[:, None]).astype(np.float32)
        cm = (np.arange(8) < c).astype(np.float32).reshape(8, 1)
        in_maps.append({
            "x_tm": xc,
            "x_fmT": np.ascontiguousarray(xc.T),
            "wqT": Wq, "wkT": Wk, "wvT": Wv, "woT": Wo,
            "bq_fm": bq_fm, "bk_fm": bk_fm,
            "bv_row": bv.reshape(1, H), "bo_row": bo.reshape(1, H),
            "lnqw2": np.ascontiguousarray(np.tile(lnqw, 2).reshape(1, 128)),
            "lnqb2": np.ascontiguousarray(np.tile(lnqb, 2).reshape(1, 128)),
            "lnkw2": np.ascontiguousarray(np.tile(lnkw, 2).reshape(1, 128)),
            "lnkb2": np.ascontiguousarray(np.tile(lnkb, 2).reshape(1, 128)),
            "gateT": gateT,
            "wgT": np.ascontiguousarray(wgTa[c * EPL:(c + 1) * EPL]),
            "wuT": np.ascontiguousarray(wuTa[c * EPL:(c + 1) * EPL]),
            "wdT": np.ascontiguousarray(wdTa[c * EPL:(c + 1) * EPL]),
            "tri_in": tri, "ident_in": ident,
            "tvals": tvals_c, "tids": tids_c, "coremask": cm,
            "lmask9": np.triu(np.ones((8, 9), np.float32), 1),
            "sel64": np.ascontiguousarray(
                np.eye(64, dtype=np.float32)[:, c * 8:(c + 1) * 8]),
            "ones_row": ones_row, "eighth_row": eighth_row, "erow_in": erow_np,
        })
    if _NC_CACHE is None:
        _NC_CACHE = build_program()
    res = run_bass_kernel_spmd(_NC_CACHE, in_maps, core_ids=list(range(NCORES)))
    out = np.empty((B, S, H), np.float32)
    for c in range(NCORES):
        b, half = c // 2, c % 2
        out[b, half * TCORE:(half + 1) * TCORE] = res.results[c]["y"]
    return out
